# version 36
# baseline (speedup 1.0000x reference)
import numpy as np
from contextlib import ExitStack

import concourse.bass as bass
import concourse.mybir as mybir
from concourse.bass_utils import run_bass_kernel_spmd

F32 = mybir.dt.float32
BF16 = mybir.dt.bfloat16
I32 = mybir.dt.int32
AF = mybir.ActivationFunctionType
ALU = mybir.AluOpType
AX = mybir.AxisListType

NSEQ = 2
S = 2048
D = 1024
NH = 16
DH = 64
NPAIR = 8
NMC = 8
NTC = 16
SF = 512
NFRAG = S // SF
NSB = 16
SCALE = 1.0 / 65536.0
EPS = 1e-5


class _Sem:
    __slots__ = ("h", "v")

    def __init__(self, h):
        self.h = h
        self.v = 0


def build_program(dumps=(), nseq=NSEQ, npair=NPAIR, nfrag=NFRAG, nsb=NSB, attn_stage="full"):
    dumps = set(dumps)
    nc = bass.Bass("TRN2", target_bir_lowering=False, debug=False)

    x_d = nc.dram_tensor("x", [NSEQ, S, D], F32, kind="ExternalInput").ap()
    xT_d = nc.dram_tensor("xT", [NSEQ, D, S], F32, kind="ExternalInput").ap()
    wq_d = nc.dram_tensor("wq", [D, D], F32, kind="ExternalInput").ap()
    wk_d = nc.dram_tensor("wk", [D, D], F32, kind="ExternalInput").ap()
    wv_d = nc.dram_tensor("wv", [D, D], F32, kind="ExternalInput").ap()
    wo_d = nc.dram_tensor("wo", [D, D], F32, kind="ExternalInput").ap()
    wf_d = nc.dram_tensor("wf", [D, D], F32, kind="ExternalInput").ap()
    fb_d = nc.dram_tensor("fb", [1, D], F32, kind="ExternalInput").ap()
    fa_d = nc.dram_tensor("fa", [128, 128], F32, kind="ExternalInput").ap()
    fbb_d = nc.dram_tensor("fbb", [16, D], F32, kind="ExternalInput").ap()
    out_d = nc.dram_tensor("out", [NSEQ, S, D], F32, kind="ExternalOutput").ap()
    scr_rd = nc.dram_tensor("scr_rd", [2, 1024], F32, kind="Internal").ap()
    scr_sc = nc.dram_tensor("scr_sc", [1, 2], F32, kind="Internal").ap()
    scr_wm = nc.dram_tensor("scr_wm", [1, 128], F32, kind="Internal").ap()

    dump_aps = {}
    for name, shape, dt in (
        ("d_qT", [128, S], BF16), ("d_kT", [128, S], BF16), ("d_vt", [128, 2080], BF16),
        ("d_E", [128, 1024], BF16), ("d_OT", [128, 16384], BF16),
        ("d_x1", [128, D], F32), ("d_x1T", [128, 1024], BF16),
        ("d_wq", [128, 1024], BF16), ("d_sclb", [128, 2], F32),
        ("d_wfq", [128, 1024], BF16), ("d_xa", [16, 128], BF16),
        ("d_x0T", [128, 2048], BF16), ("d_wmax", [128, 8], F32), ("d_gsc", [1, 4], F32),
        ("d_Bpre", [128, 1024], F32), ("d_wfall", [128, 8192], BF16),
        ("d_row", [1, 128], F32),
    ):
        if name in dumps:
            dump_aps[name] = nc.dram_tensor(name, shape, dt, kind="ExternalOutput").ap()

    ctx = ExitStack()
    with ctx:
        sb = lambda n, sh, dt: ctx.enter_context(nc.sbuf_tensor(n, sh, dt))
        ps = lambda n, sh, dt: ctx.enter_context(nc.psum_tensor(n, sh, dt))

        stg = sb("stg", [128, 2048], F32)
        wq_b = sb("wq_b", [128, 8192], BF16)
        wk_b = sb("wk_b", [128, 8192], BF16)
        wv_b = sb("wv_b", [128, 8192], BF16)
        wo_b = sb("wo_b", [128, 8192], BF16)
        wf_b = sb("wf_b", [128, 8192], BF16)
        fa_b = sb("fa_b", [128, 128], BF16)
        fbb_b = sb("fbb_b", [16, 1024], BF16)
        fbias_b = sb("fbias_b", [1, 1024], BF16)
        ones_b = sb("ones_b", [1, 128], BF16)
        ident_b = sb("ident_b", [128, 128], BF16)
        ident_f = sb("ident_f", [128, 128], F32)
        x0T = sb("x0T", [128, 16384], BF16)
        qTa = sb("qTa", [128, 2048], BF16)
        kTa = sb("kTa", [128, 2048], BF16)
        vtila = sb("vtila", [128, 2080], BF16)
        Ebuf = sb("Ebuf", [128, 2048], BF16)
        OT = sb("OT", [128, 16384], BF16)
        rd = sb("rd", [128, 2048], F32)
        rdb = sb("rdb", [64, 1024], F32)
        xabuf = sb("xabuf", [128, 128], BF16)
        x1T = sb("x1T", [128, 1024], BF16)
        vtT = x1T
        vtT2 = xabuf
        Ablk = sb("Ablk", [128, 2048], F32)
        Bblk = sb("Bblk", [128, 2048], F32)
        outblk = sb("outblk", [128, 1024], F32)
        lnsc = sb("lnsc", [128, 32], F32)
        epsc = sb("epsc", [128, 1], F32)
        wmax = sb("wmax", [128, 9], F32)
        gsc = sb("gsc", [1, 4], F32)
        sclb = sb("sclb", [128, 2], F32)

        ps0 = ps("ps0", [128, 1024], F32)
        ps1 = ps("ps1", [128, 1024], F32)
        pb = [ps(f"pb{i}", [128, 512], F32) for i in range(4)]
        ps0_bf = ps0[:].bitcast(BF16)
        otmp = rd[:].bitcast(BF16)[0:64, 0:512]
        vtilb = stg[:].bitcast(BF16)[:, 0:2080]
        qTb = Bblk[:].bitcast(BF16)[:, 0:2048]
        kTb = outblk[:].bitcast(BF16)[:, 0:2048]
        QT = [qTa[:], qTb]
        KT = [kTa[:], kTb]
        VTILE = [vtila[:], vtilb]

        sT = _Sem(ctx.enter_context(nc.semaphore("sT")))
        sV = _Sem(ctx.enter_context(nc.semaphore("sV")))
        sA = _Sem(ctx.enter_context(nc.semaphore("sA")))
        sP = _Sem(ctx.enter_context(nc.semaphore("sP")))
        dS = _Sem(ctx.enter_context(nc.semaphore("dS")))
        dG = _Sem(ctx.enter_context(nc.semaphore("dG")))
        dB = _Sem(ctx.enter_context(nc.semaphore("dB")))
        dC = _Sem(ctx.enter_context(nc.semaphore("dC")))
        oS = _Sem(ctx.enter_context(nc.semaphore("oS")))
        dS2 = _Sem(ctx.enter_context(nc.semaphore("dS2")))
        dG2 = _Sem(ctx.enter_context(nc.semaphore("dG2")))
        dC2 = _Sem(ctx.enter_context(nc.semaphore("dC2")))
        dB2 = _Sem(ctx.enter_context(nc.semaphore("dB2")))
        dB3 = _Sem(ctx.enter_context(nc.semaphore("dB3")))
        dQ = _Sem(ctx.enter_context(nc.semaphore("dQ")))

        T, V, SC, SY, G = nc.tensor, nc.vector, nc.scalar, nc.sync, nc.gpsimd
        WM = {"T": {}, "V": {}, "SC": {}, "SY": {}, "G": {}}

        def inc(instr, s, n=1):
            instr.then_inc(s.h, n)
            s.v += n
            return (s, s.v)

        def wt(eng, key, *gates):
            wm = WM[key]
            for g in gates:
                if g is None:
                    continue
                s, v = g
                if wm.get(id(s), 0) < v:
                    eng.wait_ge(s.h, v)
                    wm[id(s)] = v

        wT = lambda *g: wt(T, "T", *g)
        wV = lambda *g: wt(V, "V", *g)
        wSC = lambda *g: wt(SC, "SC", *g)
        wSY = lambda *g: wt(SY, "SY", *g)
        wG = lambda *g: wt(G, "G", *g)

        def dump(name, ap, *gates):
            if name in dump_aps:
                wSY(*gates)
                inc(SY.dma_start(dump_aps[name], ap), oS, 16)

        gi = inc(G.memset(ident_b[:], 0.0), sP)
        wG(gi)
        gi = inc(G.affine_select(out=ident_b[:], in_=ident_b[:], compare_op=ALU.not_equal,
                                 fill=1.0, base=0, pattern=[[-1, 128]], channel_multiplier=1), sP)
        gi2 = inc(G.memset(ident_f[:], 0.0), sP)
        wG(gi2)
        gi2 = inc(G.affine_select(out=ident_f[:], in_=ident_f[:], compare_op=ALU.not_equal,
                                  fill=1.0, base=0, pattern=[[-1, 128]], channel_multiplier=1), sP)
        g_init = inc(G.memset(ones_b[:], 1.0), sP)
        g_ident = g_init
        def vtile_ones(side):
            g = None
            for tc in range(NTC):
                V.memset(VTILE[side][:, tc * 130 + 64: tc * 130 + 65], 1.0)
                g = inc(V.memset(VTILE[side][:, tc * 130 + 129: tc * 130 + 130], 1.0), sV)
            return g

        vtile_ones(0)
        V.memset(epsc[:], EPS)
        g_vones = inc(V.memset(lnsc[:], 0.0), sV)

        g_lastexp = [g_vones, g_vones]
        g_ln1add = [None, None]
        g_pb_readers = [None, None, None, None]
        g_pbV = [None] * 4
        g_x0T_readers = None
        g_qkv_readers = None
        g_vtile_readers = None
        g_OT_readers = None
        g_rd_reader = None
        g_rdb_reader = [None, None]
        g_scr_slot = [None, None]
        g_pav_act = [None, None]
        g_outdma = None
        g_x1gate = [None]
        g_otmp_reader = None
        g_xa_evac = [None, None]

        r_stg = [None, None]

        def load_convert(dram_ap, dest_ap, slot, q=0):
            Q, Qk, dq = (SY, "SY", dS) if q == 0 else (G, "G", dG)
            wt(Q, Qk, r_stg[slot])
            gl = inc(Q.dma_start(stg[:, slot * 1024: slot * 1024 + dram_ap.shape[1]], dram_ap), dq, 16)
            wV(gl)
            gc = inc(V.tensor_copy(dest_ap, stg[:, slot * 1024: slot * 1024 + dram_ap.shape[1]]), sV)
            r_stg[slot] = gc
            return gc

        r_ab = [None, None]
        OT_f = OT[:].bitcast(F32)
        AB = [OT_f[:, 4096:5120], OT_f[:, 5120:6144]]
        AB2 = [OT_f[:, 6144:7168], OT_f[:, 7168:8192]]
        r_ab2 = [None, None]
        g_wmat = {"v": None, "k": None, "q": None}

        def ab_load_convert(dram_ap, dest_ap, slot):
            wG(r_ab[slot])
            gl = inc(G.dma_start(AB[slot][:, 0:dram_ap.shape[1]], dram_ap),
                     dG if slot == 0 else dG2, 16)
            wV(gl)
            gc = inc(V.tensor_copy(dest_ap, AB[slot][:, 0:dram_ap.shape[1]]), sV)
            r_ab[slot] = gc
            return gc

        g_wqkv = (sP, sP.v)
        dump("d_wq", wq_b[:, 0:1024], g_wqkv)

        def emit_rest_weights():
            wG((dG, dG.v), (dG2, dG2.v))
            for c in range(8):
                ab_load_convert(wo_d[c * 128:(c + 1) * 128, :], wo_b[:, c * 1024:(c + 1) * 1024], c % 2)
            ab_load_convert(fa_d, fa_b[:], 0)
            wG(r_ab[1])
            gl = inc(G.dma_start(AB[1][0:16, :], fbb_d), dG2, 16)
            wV(gl)
            r_ab[1] = inc(V.tensor_copy(fbb_b[:], AB[1][0:16, :]), sV)
            wG(r_ab[0])
            gl = inc(G.dma_start(AB[0][0:1, :], fb_d), dG, 16)
            wV(gl)
            r_ab[0] = inc(V.tensor_copy(fbias_b[:], AB[0][0:1, :]), sV)

        qg = {"gla": [None] * 8, "gl": [None] * 8}

        def qa_load(c, slot):
            wG(r_ab[slot])
            qg["gla"][c] = inc(G.dma_start(AB[slot][:, 0:1024],
                                           wf_d[c * 128:(c + 1) * 128, :]),
                               dG if slot == 0 else dG2, 16)

        def qa_reduce(c, slot):
            wV(qg["gla"][c])
            r_ab[slot] = inc(V.tensor_reduce(wmax[:, c:c + 1], AB[slot],
                                             AX.X, ALU.max, apply_absolute_value=True), sV)

        def emit_quant_mid():
            wV(r_ab[0], r_ab[1])
            g_wm = inc(V.tensor_reduce(wmax[:, 8:9], wmax[:, 0:8], AX.X, ALU.max), sV)
            wG(g_wm)
            g_wb1 = inc(G.dma_start(scr_wm.transpose([1, 0]), wmax[:, 8:9]), dQ, 16)
            wG(g_wb1, r_ab[0])
            g_wb2 = inc(G.dma_start(AB[0][0:1, 0:128], scr_wm), dQ, 16)
            qg["wb2"] = g_wb2
            wV(g_wb2)
            g_t1 = inc(V.tensor_reduce(gsc[0:1, 0:1], AB[0][0:1, 0:128], AX.X, ALU.max), sV)
            r_ab[0] = g_t1
            wV(g_t1)
            g_t2 = inc(V.tensor_scalar(gsc[0:1, 1:2], gsc[0:1, 0:1], 1.0 / 7.0, None, ALU.mult), sV)
            wV(g_t2)
            g_sc = inc(V.reciprocal(gsc[0:1, 2:3], gsc[0:1, 1:2]), sV)
            wG(g_sc)
            g_d1 = inc(G.dma_start(scr_sc, gsc[0:1, 1:3]), dQ, 16)
            wG(g_d1)
            g_d2 = inc(G.dma_start(sclb[:], scr_sc.broadcast_to([128, 2])), dQ, 16)
            qg["d2"] = g_d2
            dump("d_sclb", sclb[:], g_d2)
            dump("d_gsc", gsc[:], g_sc)

        def qs_load(c, slot):
            wG(r_ab[0], r_ab[1], qg["wb2"])
            qg["gl"][c] = inc(G.dma_start(AB[0],
                                          wf_d[c * 128:(c + 1) * 128, :]), dG, 16)

        def qs_scale(c, slot):
            qs = AB[1]
            qs_i = AB[0].bitcast(I32)
            wV(qg["gl"][c], qg["d2"])
            gq = inc(V.tensor_scalar(qs, AB[0], sclb[:, 1:2], None, ALU.mult), sV)
            wV(gq)
            gq = inc(V.tensor_copy(qs_i, qs), sV)
            wV(gq)
            gq = inc(V.tensor_scalar(qs, qs_i, 7.0, -8.0, ALU.min, ALU.max), sV)
            wV(gq)
            gq = inc(V.tensor_scalar(wf_b[:, c * 1024:(c + 1) * 1024], qs,
                                     sclb[:, 0:1], None, ALU.mult), sV)
            r_ab[0] = gq
            r_ab[1] = gq

        def quant_hook(g_, st):
            if g_ == 1:
                if st == 0:
                    wG((dG, dG.v), (dG2, dG2.v))
                    qa_load(0, 0); qa_load(1, 1)
                elif st == 1:
                    qa_reduce(0, 0); qa_reduce(1, 1); qa_load(2, 0); qa_load(3, 1)
                elif st == 2:
                    qa_reduce(2, 0); qa_reduce(3, 1); qa_load(4, 0); qa_load(5, 1)
                else:
                    qa_reduce(4, 0); qa_reduce(5, 1); qa_load(6, 0); qa_load(7, 1)
            elif g_ == 2:
                if st == 0:
                    qa_reduce(6, 0); qa_reduce(7, 1)
                    emit_quant_mid()
                    qs_load(0, 0)
                elif st == 1:
                    qs_scale(0, 0); qs_load(1, 0); qs_scale(1, 0); qs_load(2, 0)
                elif st == 2:
                    qs_scale(2, 0); qs_load(3, 0); qs_scale(3, 0); qs_load(4, 0)
                else:
                    qs_scale(4, 0); qs_load(5, 0); qs_scale(5, 0); qs_load(6, 0)
            elif g_ == 3 and st == 0:
                qs_scale(6, 0); qs_load(7, 0); qs_scale(7, 0)
                dump("d_wfq", wf_b[:, 0:1024], r_ab[0])
                dump("d_wfall", wf_b[:], r_ab[0], r_ab[1])

        dump("d_wq", wq_b[:, 0:1024], g_wqkv)

        g_qst = [[None] * 4, [None] * 4]
        g_kall = [None, None]
        g_vt_ready = [None, None]
        g_vtslot = [None, None]
        g_vtslot_rd = [None, None]

        def qkv_tile(gp, side, mat_b, dst, st2, bk):
            wT(g_wqkv, g_wmat[dst], g_x0T_ready, g_pbV[bk], g_pb_readers[bk], g_pav_act[bk // 2])
            mm = None
            for mc in range(NMC):
                mm = T.matmul(pb[bk][:],
                              mat_b[:, mc * 1024 + gp * 128: mc * 1024 + (gp + 1) * 128],
                              x0T[:, mc * 2048 + st2 * 512: mc * 2048 + (st2 + 1) * 512],
                              start=(mc == 0), stop=(mc == NMC - 1))
            gm = inc(mm, sT)
            g_pb_readers[bk] = gm
            wV(gm)
            if dst == "v":
                slot = st2 % 2
                wV(g_vtslot_rd[slot])
                ge = inc(V.tensor_copy(vtT[:, slot * 512:(slot + 1) * 512], pb[bk][:]), sV)
                g_vtslot[slot] = ge
            elif dst == "q":
                ge = inc(V.tensor_copy(QT[side][:, st2 * 512:(st2 + 1) * 512], pb[bk][:]), sV)
                g_qst[side][st2] = ge
            else:
                ge = inc(V.tensor_copy(KT[side][:, st2 * 512:(st2 + 1) * 512], pb[bk][:]), sV)
                g_kall[side] = ge
            g_pbV[bk] = ge
            return ge

        def v_transp(side, tc2, bk):
            slot = (tc2 // 4) % 2
            pos = (tc2 % 4) * 128
            pbk_bf = pb[bk][:].bitcast(BF16)
            wT(g_ident, g_vtslot[slot], g_pbV[bk], g_pb_readers[bk], g_pav_act[bk // 2])
            gtr = inc(T.transpose(pbk_bf[:, pos: pos + 128],
                                  vtT[:, slot * 512 + pos: slot * 512 + pos + 128],
                                  ident_b[:]), sT)
            g_pb_readers[bk] = gtr
            g_vtslot_rd[slot] = gtr
            wV(gtr)
            V.tensor_copy(VTILE[side][:, tc2 * 130: tc2 * 130 + 64], pbk_bf[:, pos: pos + 64])
            gcv = inc(V.tensor_copy(VTILE[side][:, tc2 * 130 + 65: tc2 * 130 + 129],
                                    pbk_bf[:, pos + 64: pos + 128]), sV)
            g_pbV[bk] = gcv
            g_vt_ready[side] = gcv
            return gcv

        def window_items(gp, side, st):
            W0, W1 = 2 * (1 - (st % 2)), 2 * (1 - (st % 2)) + 1
            if st == 0:
                pre = [(vtile_ones, (side,))] if side == 1 else []
                return pre + [(qkv_tile, (gp, side, wv_b, "v", 0, W0)),
                        (qkv_tile, (gp, side, wv_b, "v", 1, W1)),
                        (qkv_tile, (gp, side, wk_b, "k", 0, W0))]
            if st == 1:
                return [(v_transp, (side, 0, W1)), (v_transp, (side, 1, W1)),
                        (v_transp, (side, 2, W1)), (v_transp, (side, 3, W1)),
                        (qkv_tile, (gp, side, wv_b, "v", 2, W0)),
                        (qkv_tile, (gp, side, wk_b, "k", 1, W1)),
                        (v_transp, (side, 4, W0)), (v_transp, (side, 5, W0)),
                        (v_transp, (side, 6, W0)), (v_transp, (side, 7, W0))]
            if st == 2:
                return [(qkv_tile, (gp, side, wv_b, "v", 3, W0)),
                        (qkv_tile, (gp, side, wk_b, "k", 2, W1)),
                        (qkv_tile, (gp, side, wk_b, "k", 3, W0)),
                        (v_transp, (side, 8, W1)), (v_transp, (side, 9, W1)),
                        (v_transp, (side, 10, W1)), (v_transp, (side, 11, W1))]
            return [(v_transp, (side, 12, W1)), (v_transp, (side, 13, W1)),
                    (v_transp, (side, 14, W1)), (v_transp, (side, 15, W1)),
                    (qkv_tile, (gp, side, wq_b, "q", 0, W0)),
                    (qkv_tile, (gp, side, wq_b, "q", 1, W0)),
                    (qkv_tile, (gp, side, wq_b, "q", 2, W1)),
                    (qkv_tile, (gp, side, wq_b, "q", 3, W0))]

        for b in range(nseq):
            wV((oS, oS.v))
            if b > 0:
                wG((oS, oS.v))
            g_xchunk = [None] * NMC
            qsi = {"SY": 0, "G": 0, "SC": 0}

            def q_issue(qn, dram_ap, extra=()):
                sl = qsi[qn] % 2
                qsi[qn] += 1
                w = dram_ap.shape[1]
                if qn == "SY":
                    wSY(r_stg[sl], *extra)
                    gl = inc(SY.dma_start(stg[:, sl * 1024: sl * 1024 + w], dram_ap),
                             dS if sl == 0 else dS2, 16)
                elif qn == "G":
                    wG(r_ab2[sl], *extra)
                    gl = inc(G.dma_start(AB2[sl][:, 0:w], dram_ap),
                             dG if sl == 0 else dG2, 16)
                else:
                    wSC(r_ab[sl], *extra)
                    gl = inc(SC.dma_start(AB[sl][:, 0:w], dram_ap),
                             dC if sl == 0 else dC2, 16)
                return gl, sl

            def q_conv(qn, gl, sl, dest_ap, extra=()):
                wV(gl, *extra)
                w = dest_ap.shape[1]
                src = (stg[:, sl * 1024: sl * 1024 + w] if qn == "SY"
                       else (AB2 if qn == "G" else AB)[sl][:, 0:w])
                gc = inc(V.tensor_copy(dest_ap, src), sV)
                (r_stg if qn == "SY" else (r_ab2 if qn == "G" else r_ab))[sl] = gc
                return gc

            for mc in range(NMC):
                gl0, s0 = q_issue("SY", xT_d[b, mc * 128:(mc + 1) * 128, 0:1024],
                                  (g_vtile_readers,))
                gl1, s1 = q_issue("G", xT_d[b, mc * 128:(mc + 1) * 128, 1024:2048])
                if b == 0:
                    gl2, s2 = q_issue("SC", wv_d[mc * 128:(mc + 1) * 128, :])
                q_conv("SY", gl0, s0, x0T[:, mc * 2048: mc * 2048 + 1024],
                       (g_x0T_readers,))
                gc = q_conv("G", gl1, s1, x0T[:, mc * 2048 + 1024:(mc + 1) * 2048],
                            (g_x0T_readers,))
                if b == 0:
                    gc = q_conv("SC", gl2, s2, wv_b[:, mc * 1024:(mc + 1) * 1024])
                g_xchunk[mc] = gc
                wT(g_wqkv, g_xchunk[mc], g_qkv_readers, g_vtile_readers)
                for st2 in range(4):
                    if mc == 0:
                        wT(g_pbV[st2], g_pb_readers[st2], g_pav_act[st2 // 2])
                    mm = T.matmul(pb[st2][:],
                                  wv_b[:, mc * 1024: mc * 1024 + 128],
                                  x0T[:, mc * 2048 + st2 * 512: mc * 2048 + (st2 + 1) * 512],
                                  start=(mc == 0), stop=(mc == NMC - 1),
                                  skip_group_check=True)
                    if mc == NMC - 1:
                        gm = inc(mm, sT)
                        g_pb_readers[st2] = gm
                        if st2 < 2:
                            wV(gm, g_vtslot_rd[st2])
                            ge = inc(V.tensor_copy(vtT[:, st2 * 512:(st2 + 1) * 512],
                                                   pb[st2][:]), sV)
                            g_vtslot[st2] = ge
                            g_pbV[st2] = ge
            g_x0T_ready = (sV, sV.v)
            if b == 0:
                dump("d_x0T", x0T[:, 0:2048], g_x0T_ready)
                for qn, c in (("SY", 0), ("G", 1), ("SC", 6), ("SY", 2), ("G", 3),
                              ("SC", 7), ("SY", 4), ("G", 5)):
                    gl, sl = q_issue(qn, wk_d[c * 128:(c + 1) * 128, :])
                    q_conv(qn, gl, sl, wk_b[:, c * 1024:(c + 1) * 1024])
                g_wmat["k"] = (sV, sV.v)
                for qn, c in (("SY", 0), ("G", 1), ("SC", 6), ("SY", 2), ("G", 3),
                              ("SC", 7), ("SY", 4), ("G", 5)):
                    gl, sl = q_issue(qn, wq_d[c * 128:(c + 1) * 128, :])
                    q_conv(qn, gl, sl, wq_b[:, c * 1024:(c + 1) * 1024])
                g_wmat["q"] = (sV, sV.v)

            bkc = [0]

            def nextbk():
                bkc[0] = (bkc[0] + 1) % 4
                return bkc[0]

            for tc2 in range(8):
                v_transp(0, tc2, tc2 % 2)
            for st2 in (2, 3):
                wV((sT, sT.v), g_vtslot_rd[st2 % 2])
                ge = inc(V.tensor_copy(vtT[:, (st2 % 2) * 512:(st2 % 2 + 1) * 512],
                                       pb[st2][:]), sV)
                g_vtslot[st2 % 2] = ge
                g_pbV[st2] = ge
            for tc2 in range(8, 16):
                v_transp(0, tc2, nextbk())
            for st2 in range(4):
                qkv_tile(0, 0, wk_b, "k", st2, nextbk())
            for st2 in range(4):
                qkv_tile(0, 0, wq_b, "q", st2, nextbk())
            if b == 0:
                dump("d_qT", QT[0], g_qst[0][3])
                dump("d_kT", KT[0], g_kall[0])
                dump("d_vt", VTILE[0], g_vt_ready[0])

            pend_norm = [None, None]
            for g in range(npair):
                side = g % 2
                qT_s, kT_s, vtile_s = QT[side], KT[side], VTILE[side]

                def emit_normalize(st, g_=None, gates=None):
                    g2 = g if g_ is None else g_
                    fp = st % 2
                    pav0, pav1 = pb[2 * fp], pb[2 * fp + 1]
                    g_av = (gates or g_av_gates)[st]
                    nonlocal g_rd_reader, g_otmp_reader
                    wSC(g_av, g_rd_reader)
                    SC.activation(rd[64:65, 0:512], pav0[64:65, :], AF.Ln)
                    g_ln = inc(SC.activation(rd[64:65, 512:1024], pav1[64:65, :], AF.Ln), sA)
                    wSC(g_ln)
                    g_rd = inc(SC.activation(rd[64:65, 1024:2048], rd[64:65, 0:1024],
                                             AF.Exp, scale=-1.0), sA)
                    g_pav_act[fp] = g_rd
                    wSY(g_rd, g_scr_slot[st % 2])
                    g_b1 = inc(SY.dma_start(scr_rd[st % 2:st % 2 + 1, :], rd[64:65, 1024:2048]), dB, 16)
                    g_rd_reader = g_b1
                    wSY(g_b1, g_rdb_reader[0])
                    g_b2 = inc(SY.dma_start(rdb[:, 0:1024],
                                            scr_rd[st % 2:st % 2 + 1, :].broadcast_to([64, 1024])), dB3, 16)
                    g_scr_slot[st % 2] = g_b2
                    wV(g_b2, g_OT_readers, qg.get("done"), r_ab[0], r_ab[1])
                    gm0 = inc(V.tensor_tensor(OT[0:64, g2 * 2048 + st * 512: g2 * 2048 + (st + 1) * 512],
                                              pav0[0:64, :], rdb[:, 0:512],
                                              ALU.mult), sV)
                    wV(g_otmp_reader)
                    gm1 = inc(V.tensor_tensor(otmp, pav1[0:64, :],
                                              rdb[:, 512:1024], ALU.mult), sV)
                    g_rdb_reader[0] = gm1
                    g_pbV[2 * fp] = gm1
                    g_pbV[2 * fp + 1] = gm1
                    wSY(gm1)
                    g_mv = inc(SY.dma_start(OT[64:128, g2 * 2048 + st * 512: g2 * 2048 + (st + 1) * 512],
                                            otmp), dB2, 16)
                    g_otmp_reader = g_mv

                g_av_gates = [None] * nfrag
                NI = nfrag * NTC
                g_sc_gates = [None] * (NI + 1)
                g_exp_gates = [None] * NI
                witems = []
                wpos = 0

                def emit_scores(i):
                    st, p = i // NTC, i % 2
                    P = ps0 if p == 0 else ps1
                    wT(g_lastexp[p], g_kall[side], g_qst[side][st])
                    T.matmul(P[:, 0:512],
                             kT_s[0:64, (i % NTC) * 128:(i % NTC + 1) * 128],
                             qT_s[0:64, st * 512:(st + 1) * 512],
                             start=True, stop=True, tile_position=(0, 0))
                    mm = T.matmul(P[:, 512:1024],
                                  kT_s[64:128, (i % NTC) * 128:(i % NTC + 1) * 128],
                                  qT_s[64:128, st * 512:(st + 1) * 512],
                                  start=True, stop=True, tile_position=(64, 0))
                    g_sc_gates[i] = inc(mm, sT)

                def emit_av(i):
                    st, tc, p = i // NTC, i % NTC, i % 2
                    fp = st % 2
                    pav0, pav1 = pb[2 * fp], pb[2 * fp + 1]
                    wT(g_exp_gates[i], g_vt_ready[side])
                    if tc == 0:
                        wT(g_pbV[2 * fp], g_pbV[2 * fp + 1], g_pav_act[fp],
                           g_pb_readers[2 * fp], g_pb_readers[2 * fp + 1])
                    T.matmul(pav0[0:65, :], vtile_s[:, tc * 130: tc * 130 + 65],
                             Ebuf[:, p * 1024: p * 1024 + 512],
                             start=(tc == 0), stop=(tc == NTC - 1), skip_group_check=True)
                    mm = T.matmul(pav1[0:65, :], vtile_s[:, tc * 130 + 65: tc * 130 + 130],
                                  Ebuf[:, p * 1024 + 512: (p + 1) * 1024],
                                  start=(tc == 0), stop=(tc == NTC - 1), skip_group_check=True)
                    if tc == NTC - 1:
                        g_av_gates[st] = inc(mm, sT)

                emit_scores(0)
                for i in range(NI):
                    st, tc, p = i // NTC, i % NTC, i % 2
                    if tc == 0 and g + 1 < npair:
                        witems.extend(window_items(g + 1, 1 - side, st))
                    P = ps0 if p == 0 else ps1
                    wSC(g_sc_gates[i])
                    g_exp_gates[i] = inc(
                        SC.activation(Ebuf[:, p * 1024:(p + 1) * 1024], P[:, 0:1024],
                                      AF.Exp, scale=SCALE), sA)
                    g_lastexp[p] = g_exp_gates[i]
                    if i >= 1:
                        emit_av(i - 1)
                    if i + 1 < NI:
                        emit_scores(i + 1)
                    if tc == 1:
                        if st > 0:
                            emit_normalize(st - 1)
                        elif pend_norm[0] is not None:
                            emit_normalize(nfrag - 1, g_=pend_norm[0], gates=pend_norm[1])
                            pend_norm[0] = None
                    if tc == 4 and b == 0:
                        quant_hook(g, st)
                    if tc >= 10 and wpos < len(witems):
                        for _ in range(2):
                            if wpos < len(witems):
                                fn, args = witems[wpos]
                                fn(*args)
                                wpos += 1
                    if tc == NTC - 1:
                        assert wpos == len(witems), (g, st, wpos, len(witems))
                emit_av(NI - 1)
                while wpos < len(witems):
                    fn, args = witems[wpos]
                    fn(*args)
                    wpos += 1
                fp = (nfrag - 1) % 2
                g_pb_readers[2 * fp] = g_av_gates[nfrag - 1]
                g_pb_readers[2 * fp + 1] = g_av_gates[nfrag - 1]
                if b == 0 and g == 0:
                    dump("d_E", Ebuf[:, 1024:2048], g_exp_gates[NTC - 1])
                if g + 1 < npair:
                    pend_norm[0] = g
                    pend_norm[1] = g_av_gates
                else:
                    emit_normalize(nfrag - 1)
                g_qkv_readers = (sT, sT.v)
                g_vtile_readers = (sT, sT.v)
                if b == 0 and g == 0:
                    emit_rest_weights()
            g_OT_ready_V = (sV, sV.v)
            g_OT_ready_G = (dG, dG.v)
            g_norm_dB = (dB, dB.v)
            g_norm_dB3 = (dB3, dB3.v)
            g_norm_dB2 = (dB2, dB2.v)
            if b == 0:
                dump("d_OT", OT[:], g_OT_ready_V, g_OT_ready_G, g_norm_dB)

            sq_scr = Ebuf[:].bitcast(F32)
            g_wo_gates = [None] * nsb
            g_xr_gates = [None] * nsb

            def xres_prefetch(sbk):
                p = sbk % 2
                wSY(r_stg[p], g_vtile_readers)
                g_xr_gates[sbk] = inc(SY.dma_start(stg[:, p * 1024:(p + 1) * 1024],
                                                   x_d[b, sbk * 128:(sbk + 1) * 128, :]),
                                      dS if p == 0 else dS2, 16)

            def part1(sbk):
                p = sbk % 2
                P = ps0 if p == 0 else ps1
                if sbk == 0:
                    xres_prefetch(0)
                if sbk + 1 < nsb:
                    xres_prefetch(sbk + 1)
                wT(g_OT_ready_V)
                wt(T, "T", g_OT_ready_G)
                wt(T, "T", g_norm_dB)
                wt(T, "T", g_norm_dB3)
                wt(T, "T", g_norm_dB2)
                wT(g_lastexp[p], g_ln1add[p])
                for fc in range(8):
                    for half in range(2):
                        mm = T.matmul(P[:, half * 512:(half + 1) * 512],
                                      OT[:, fc * 2048 + sbk * 128: fc * 2048 + (sbk + 1) * 128],
                                      wo_b[:, fc * 1024 + half * 512: fc * 1024 + (half + 1) * 512],
                                      start=(fc == 0), stop=(fc == 7), skip_group_check=True)
                g_wo_gates[sbk] = inc(mm, sT)
                wV(g_wo_gates[sbk], g_xr_gates[sbk], qg.get("done"), r_ab[0], r_ab[1])
                g_add1 = inc(V.scalar_tensor_tensor(Bblk[:, p * 1024:(p + 1) * 1024], P[:],
                                                    1.0, stg[:, p * 1024:(p + 1) * 1024],
                                                    ALU.mult, ALU.add,
                                                    accum_out=lnsc[:, p * 8: p * 8 + 1]), sV)
                g_ln1add[p] = g_add1
                r_stg[p] = g_add1
                wV(g_add1)
                g_nm = inc(V.tensor_scalar(lnsc[:, p * 8 + 1: p * 8 + 2], lnsc[:, p * 8: p * 8 + 1],
                                           -1.0 / D, None, ALU.mult), sV)
                wSC(g_nm)
                SC.activation(sq_scr[:, 0:1024], Bblk[:, p * 1024:(p + 1) * 1024], AF.Square,
                              bias=lnsc[:, p * 8 + 1: p * 8 + 2],
                              accum_out=lnsc[:, p * 8 + 2: p * 8 + 3])
                SC.activation(lnsc[:, p * 8 + 3: p * 8 + 4], lnsc[:, p * 8 + 2: p * 8 + 3],
                              AF.Ln, bias=epsc[:, 0:1], scale=1.0 / D)
                g_rstd = inc(SC.activation(lnsc[:, p * 8 + 4: p * 8 + 5], lnsc[:, p * 8 + 3: p * 8 + 4],
                                           AF.Exp, scale=-0.5), sA)
                return (g_nm, g_rstd)

            def part2a(sbk, g_nm, g_rstd):
                p = sbk % 2
                wV(g_rstd, g_nm)
                g_x1 = inc(V.tensor_scalar(Ablk[:, p * 1024:(p + 1) * 1024],
                                           Bblk[:, p * 1024:(p + 1) * 1024],
                                           lnsc[:, p * 8 + 1: p * 8 + 2],
                                           lnsc[:, p * 8 + 4: p * 8 + 5],
                                           ALU.add, ALU.mult), sV)
                return g_x1

            def part2b(sbk, g_x1):
                p = sbk % 2
                for mc in range(8):
                    bank = pb[mc // 4]
                    if mc % 4 == 0:
                        wT(g_ident, g_x1, g_pbV[mc // 4], g_pb_readers[mc // 4], g_xa_evac[0],
                           g_pav_act[0])
                    gtr = inc(T.transpose(bank[:, (mc % 4) * 128:(mc % 4 + 1) * 128],
                                          Ablk[:, p * 1024 + mc * 128: p * 1024 + (mc + 1) * 128],
                                          ident_f[:]), sT)
                    g_pb_readers[mc // 4] = gtr
                wV(gtr)
                V.tensor_copy(x1T[:, 0:512], pb[0][:])
                g_x1T = inc(V.tensor_copy(x1T[:, 512:1024], pb[1][:]), sV)
                g_pbV[0] = g_x1T
                g_pbV[1] = g_x1T
                wT(g_x1T, g_pbV[2], g_pb_readers[2], g_pav_act[1])
                mm = None
                for mc in range(8):
                    mm = T.matmul(pb[2][0:16, 0:128], fa_b[:, mc * 16:(mc + 1) * 16],
                                  x1T[:, mc * 128:(mc + 1) * 128],
                                  start=(mc == 0), stop=(mc == 7))
                g_xamm = inc(mm, sT)
                g_pb_readers[2] = g_xamm
                wV(g_xamm)
                g_xa = inc(V.tensor_copy(xabuf[0:16, :], pb[2][0:16, 0:128]), sV)
                g_xa_evac[0] = g_xa
                g_pbV[2] = g_xa
                if b == 0 and sbk == 0:
                    dump("d_x1", Ablk[:, 0:1024], g_x1)
                    dump("d_x1T", x1T[:], g_x1T)
                    dump("d_xa", xabuf[0:16, :], g_xa)
                for half in range(2):
                    bank = pb[2 + half]
                    wT(g_ident, g_xa, g_pbV[2 + half], g_pb_readers[2 + half])
                    mm = None
                    for mc in range(8):
                        mm = T.matmul(bank[:], x1T[:, mc * 128:(mc + 1) * 128],
                                      wf_b[:, mc * 1024 + half * 512: mc * 1024 + (half + 1) * 512],
                                      start=(mc == 0), stop=False)
                    mm = T.matmul(bank[:], xabuf[0:16, :],
                                  fbb_b[:, half * 512:(half + 1) * 512], start=False, stop=False)
                    mm = T.matmul(bank[:], ones_b[:], fbias_b[:, half * 512:(half + 1) * 512],
                                  start=False, stop=True)
                    gf = inc(mm, sT)
                    g_pb_readers[2 + half] = gf
                    wV(gf)
                    ga = inc(V.scalar_tensor_tensor(
                        Bblk[:, p * 1024 + half * 512: p * 1024 + (half + 1) * 512], bank[:],
                        1.0, Ablk[:, p * 1024 + half * 512: p * 1024 + (half + 1) * 512],
                        ALU.mult, ALU.add,
                        accum_out=lnsc[:, p * 8 + 5 + half: p * 8 + 6 + half]), sV)
                    g_pbV[2 + half] = ga
                g_x1gate[0] = ga
                wV(ga)
                g_s2 = inc(V.tensor_tensor(lnsc[:, p * 8: p * 8 + 1], lnsc[:, p * 8 + 5: p * 8 + 6],
                                           lnsc[:, p * 8 + 6: p * 8 + 7], ALU.add), sV)
                wV(g_s2)
                g_nm2 = inc(V.tensor_scalar(lnsc[:, p * 8 + 1: p * 8 + 2], lnsc[:, p * 8: p * 8 + 1],
                                            -1.0 / D, None, ALU.mult), sV)
                wSC(g_nm2)
                SC.activation(sq_scr[:, 0:1024], Bblk[:, p * 1024:(p + 1) * 1024], AF.Square,
                              bias=lnsc[:, p * 8 + 1: p * 8 + 2],
                              accum_out=lnsc[:, p * 8 + 2: p * 8 + 3])
                SC.activation(lnsc[:, p * 8 + 3: p * 8 + 4], lnsc[:, p * 8 + 2: p * 8 + 3],
                              AF.Ln, bias=epsc[:, 0:1], scale=1.0 / D)
                g_rstd2 = inc(SC.activation(lnsc[:, p * 8 + 4: p * 8 + 5],
                                            lnsc[:, p * 8 + 3: p * 8 + 4], AF.Exp, scale=-0.5), sA)
                return (g_nm2, g_rstd2)

            def part3(sbk, g_nm2, g_rstd2):
                nonlocal g_outdma
                p = sbk % 2
                wV(g_rstd2, g_nm2, g_qkv_readers, g_outdma)
                g_x2 = inc(V.tensor_scalar(outblk[:], Bblk[:, p * 1024:(p + 1) * 1024],
                                           lnsc[:, p * 8 + 1: p * 8 + 2],
                                           lnsc[:, p * 8 + 4: p * 8 + 5],
                                           ALU.add, ALU.mult), sV)
                wSY(g_x2)
                g_outdma = inc(SY.dma_start(out_d[b, sbk * 128:(sbk + 1) * 128, :],
                                            outblk[:]), oS, 16)

            q2b = []
            q3 = []
            for sbk in range(nsb):
                if q3:
                    part3(*q3.pop(0))
                h = part1(sbk)
                gx1 = part2a(sbk, *h)
                if q2b:
                    k2, g2 = q2b.pop(0)
                    q3.append((k2,) + part2b(k2, g2))
                q2b.append((sbk, gx1))
            while q2b:
                if q3:
                    part3(*q3.pop(0))
                k2, g2 = q2b.pop(0)
                q3.append((k2,) + part2b(k2, g2))
            while q3:
                part3(*q3.pop(0))
            g_lastexp[0] = g_ln1add[0]
            g_lastexp[1] = g_ln1add[1]
            g_OT_readers = (sT, sT.v)
            g_x0T_readers = (sT, sT.v)
        SY.wait_ge(oS.h, oS.v)
    return nc


def _host_layout(L0_attn_x, L0_attn_kernel, L0_attn_WO, L0_ffn_kernel, L0_ffn_bias,
                 L0_ffn_a, L0_ffn_b):
    x = np.ascontiguousarray(np.asarray(L0_attn_x, dtype=np.float32))
    kern = np.asarray(L0_attn_kernel, dtype=np.float32)
    wq = np.ascontiguousarray(kern[0].transpose(1, 0, 2).reshape(D, D))
    wk = np.ascontiguousarray(kern[1].transpose(1, 0, 2).reshape(D, D))
    wv = np.ascontiguousarray(kern[2].transpose(1, 0, 2).reshape(D, D))
    wo = np.ascontiguousarray(np.asarray(L0_attn_WO, dtype=np.float32)[0]
                              .reshape(DH, NH, D).transpose(1, 0, 2).reshape(D, D))
    wf = np.ascontiguousarray(np.asarray(L0_ffn_kernel, dtype=np.float32))
    fb = np.ascontiguousarray(np.asarray(L0_ffn_bias, dtype=np.float32).reshape(1, D))
    fa = np.ascontiguousarray(np.asarray(L0_ffn_a, dtype=np.float32)
                              .reshape(8, 128, 16).transpose(1, 0, 2).reshape(128, 128))
    fbb = np.ascontiguousarray(np.asarray(L0_ffn_b, dtype=np.float32))
    shared = dict(wq=wq, wk=wk, wv=wv, wo=wo, wf=wf, fb=fb, fa=fa, fbb=fbb)
    in_maps = []
    for c in range(8):
        xs = np.ascontiguousarray(x[2 * c: 2 * c + 2])
        xTs = np.ascontiguousarray(xs.transpose(0, 2, 1))
        in_maps.append(dict(x=xs, xT=xTs, **shared))
    return in_maps


def kernel(L0_attn_x, L0_attn_kernel, L0_attn_WO, L0_ffn_kernel, L0_ffn_bias,
           L0_ffn_a, L0_ffn_b):
    in_maps = _host_layout(L0_attn_x, L0_attn_kernel, L0_attn_WO, L0_ffn_kernel,
                           L0_ffn_bias, L0_ffn_a, L0_ffn_b)
    nc = build_program()
    res = run_bass_kernel_spmd(nc, in_maps, core_ids=list(range(8)))
    return np.concatenate([r["out"] for r in res.results], axis=0)


# version 37
# speedup vs baseline: 1.0091x; 1.0091x over previous
import numpy as np
from contextlib import ExitStack

import concourse.bass as bass
import concourse.mybir as mybir
from concourse.bass_utils import run_bass_kernel_spmd

F32 = mybir.dt.float32
BF16 = mybir.dt.bfloat16
I32 = mybir.dt.int32
AF = mybir.ActivationFunctionType
ALU = mybir.AluOpType
AX = mybir.AxisListType

NSEQ = 2
S = 2048
D = 1024
NH = 16
DH = 64
NPAIR = 8
NMC = 8
NTC = 16
SF = 512
NFRAG = S // SF
NSB = 16
SCALE = 1.0 / 65536.0
EPS = 1e-5


class _Sem:
    __slots__ = ("h", "v")

    def __init__(self, h):
        self.h = h
        self.v = 0


def build_program(dumps=(), nseq=NSEQ, npair=NPAIR, nfrag=NFRAG, nsb=NSB, attn_stage="full"):
    dumps = set(dumps)
    nc = bass.Bass("TRN2", target_bir_lowering=False, debug=False)

    x_d = nc.dram_tensor("x", [NSEQ, S, D], F32, kind="ExternalInput").ap()
    xT_d = nc.dram_tensor("xT", [NSEQ, D, S], F32, kind="ExternalInput").ap()
    wq_d = nc.dram_tensor("wq", [D, D], F32, kind="ExternalInput").ap()
    wk_d = nc.dram_tensor("wk", [D, D], F32, kind="ExternalInput").ap()
    wv_d = nc.dram_tensor("wv", [D, D], F32, kind="ExternalInput").ap()
    wo_d = nc.dram_tensor("wo", [D, D], F32, kind="ExternalInput").ap()
    wf_d = nc.dram_tensor("wf", [D, D], F32, kind="ExternalInput").ap()
    fb_d = nc.dram_tensor("fb", [1, D], F32, kind="ExternalInput").ap()
    fa_d = nc.dram_tensor("fa", [128, 128], F32, kind="ExternalInput").ap()
    fbb_d = nc.dram_tensor("fbb", [16, D], F32, kind="ExternalInput").ap()
    out_d = nc.dram_tensor("out", [NSEQ, S, D], F32, kind="ExternalOutput").ap()
    scr_rd = nc.dram_tensor("scr_rd", [2, 1024], F32, kind="Internal").ap()
    scr_sc = nc.dram_tensor("scr_sc", [1, 2], F32, kind="Internal").ap()
    scr_wm = nc.dram_tensor("scr_wm", [1, 128], F32, kind="Internal").ap()

    dump_aps = {}
    for name, shape, dt in (
        ("d_qT", [128, S], BF16), ("d_kT", [128, S], BF16), ("d_vt", [128, 2080], BF16),
        ("d_E", [128, 1024], BF16), ("d_OT", [128, 16384], BF16),
        ("d_x1", [128, D], F32), ("d_x1T", [128, 1024], BF16),
        ("d_wq", [128, 1024], BF16), ("d_sclb", [128, 2], F32),
        ("d_wfq", [128, 1024], BF16), ("d_xa", [16, 128], BF16),
        ("d_x0T", [128, 2048], BF16), ("d_wmax", [128, 8], F32), ("d_gsc", [1, 4], F32),
        ("d_Bpre", [128, 1024], F32), ("d_wfall", [128, 8192], BF16),
        ("d_row", [1, 128], F32),
    ):
        if name in dumps:
            dump_aps[name] = nc.dram_tensor(name, shape, dt, kind="ExternalOutput").ap()

    ctx = ExitStack()
    with ctx:
        sb = lambda n, sh, dt: ctx.enter_context(nc.sbuf_tensor(n, sh, dt))
        ps = lambda n, sh, dt: ctx.enter_context(nc.psum_tensor(n, sh, dt))

        stg = sb("stg", [128, 2048], F32)
        wq_b = sb("wq_b", [128, 8192], BF16)
        wk_b = sb("wk_b", [128, 8192], BF16)
        wv_b = sb("wv_b", [128, 8192], BF16)
        wo_b = sb("wo_b", [128, 8192], BF16)
        wf_b = sb("wf_b", [128, 8192], BF16)
        fa_b = sb("fa_b", [128, 128], BF16)
        fbb_b = sb("fbb_b", [16, 1024], BF16)
        fbias_b = sb("fbias_b", [1, 1024], BF16)
        ones_b = sb("ones_b", [1, 128], BF16)
        ident_b = sb("ident_b", [128, 128], BF16)
        ident_f = sb("ident_f", [128, 128], F32)
        x0T = sb("x0T", [128, 16384], BF16)
        qTa = sb("qTa", [128, 2048], BF16)
        kTa = sb("kTa", [128, 2048], BF16)
        vtila = sb("vtila", [128, 2080], BF16)
        Ebuf = sb("Ebuf", [128, 2048], BF16)
        OT = sb("OT", [128, 16384], BF16)
        rd = sb("rd", [128, 2048], F32)
        rdb = sb("rdb", [64, 1024], F32)
        xabuf = sb("xabuf", [128, 128], BF16)
        x1T = sb("x1T", [128, 1024], BF16)
        vtT = x1T
        vtT2 = xabuf
        Ablk = sb("Ablk", [128, 2048], F32)
        Bblk = sb("Bblk", [128, 2048], F32)
        outblk = sb("outblk", [128, 1024], F32)
        lnsc = sb("lnsc", [128, 32], F32)
        epsc = sb("epsc", [128, 1], F32)
        wmax = sb("wmax", [128, 9], F32)
        gsc = sb("gsc", [1, 4], F32)
        sclb = sb("sclb", [128, 2], F32)

        ps0 = ps("ps0", [128, 1024], F32)
        ps1 = ps("ps1", [128, 1024], F32)
        pb = [ps(f"pb{i}", [128, 512], F32) for i in range(4)]
        ps0_bf = ps0[:].bitcast(BF16)
        otmp = rd[:].bitcast(BF16)[0:64, 0:512]
        vtilb = stg[:].bitcast(BF16)[:, 0:2080]
        qTb = Bblk[:].bitcast(BF16)[:, 0:2048]
        kTb = outblk[:].bitcast(BF16)[:, 0:2048]
        QT = [qTa[:], qTb]
        KT = [kTa[:], kTb]
        VTILE = [vtila[:], vtilb]

        sT = _Sem(ctx.enter_context(nc.semaphore("sT")))
        sV = _Sem(ctx.enter_context(nc.semaphore("sV")))
        sA = _Sem(ctx.enter_context(nc.semaphore("sA")))
        sP = _Sem(ctx.enter_context(nc.semaphore("sP")))
        dS = _Sem(ctx.enter_context(nc.semaphore("dS")))
        dG = _Sem(ctx.enter_context(nc.semaphore("dG")))
        dB = _Sem(ctx.enter_context(nc.semaphore("dB")))
        dC = _Sem(ctx.enter_context(nc.semaphore("dC")))
        oS = _Sem(ctx.enter_context(nc.semaphore("oS")))
        dS2 = _Sem(ctx.enter_context(nc.semaphore("dS2")))
        dG2 = _Sem(ctx.enter_context(nc.semaphore("dG2")))
        dC2 = _Sem(ctx.enter_context(nc.semaphore("dC2")))
        dB2 = _Sem(ctx.enter_context(nc.semaphore("dB2")))
        dB3 = _Sem(ctx.enter_context(nc.semaphore("dB3")))
        dQ = _Sem(ctx.enter_context(nc.semaphore("dQ")))

        T, V, SC, SY, G = nc.tensor, nc.vector, nc.scalar, nc.sync, nc.gpsimd
        WM = {"T": {}, "V": {}, "SC": {}, "SY": {}, "G": {}}

        def inc(instr, s, n=1):
            instr.then_inc(s.h, n)
            s.v += n
            return (s, s.v)

        def wt(eng, key, *gates):
            wm = WM[key]
            for g in gates:
                if g is None:
                    continue
                s, v = g
                if wm.get(id(s), 0) < v:
                    eng.wait_ge(s.h, v)
                    wm[id(s)] = v

        wT = lambda *g: wt(T, "T", *g)
        wV = lambda *g: wt(V, "V", *g)
        wSC = lambda *g: wt(SC, "SC", *g)
        wSY = lambda *g: wt(SY, "SY", *g)
        wG = lambda *g: wt(G, "G", *g)

        def dump(name, ap, *gates):
            if name in dump_aps:
                wSY(*gates)
                inc(SY.dma_start(dump_aps[name], ap), oS, 16)

        gi = inc(G.memset(ident_b[:], 0.0), sP)
        wG(gi)
        gi = inc(G.affine_select(out=ident_b[:], in_=ident_b[:], compare_op=ALU.not_equal,
                                 fill=1.0, base=0, pattern=[[-1, 128]], channel_multiplier=1), sP)
        gi2 = inc(G.memset(ident_f[:], 0.0), sP)
        wG(gi2)
        gi2 = inc(G.affine_select(out=ident_f[:], in_=ident_f[:], compare_op=ALU.not_equal,
                                  fill=1.0, base=0, pattern=[[-1, 128]], channel_multiplier=1), sP)
        g_init = inc(G.memset(ones_b[:], 1.0), sP)
        g_ident = g_init
        def vtile_ones(side):
            g = None
            for tc in range(NTC):
                V.memset(VTILE[side][:, tc * 130 + 64: tc * 130 + 65], 1.0)
                g = inc(V.memset(VTILE[side][:, tc * 130 + 129: tc * 130 + 130], 1.0), sV)
            return g

        vtile_ones(0)
        V.memset(epsc[:], EPS)
        g_vones = inc(V.memset(lnsc[:], 0.0), sV)

        g_lastexp = [g_vones, g_vones]
        g_ln1add = [None, None]
        g_pb_readers = [None, None, None, None]
        g_pbV = [None] * 4
        g_x0T_readers = None
        g_qkv_readers = None
        g_vtile_readers = None
        g_OT_readers = None
        g_rd_reader = None
        g_rdb_reader = [None, None]
        g_scr_slot = [None, None]
        g_pav_act = [None, None]
        g_outdma = None
        g_x1gate = [None]
        g_otmp_reader = None
        g_xa_evac = [None, None]

        r_stg = [None, None]

        def load_convert(dram_ap, dest_ap, slot, q=0):
            Q, Qk, dq = (SY, "SY", dS) if q == 0 else (G, "G", dG)
            wt(Q, Qk, r_stg[slot])
            gl = inc(Q.dma_start(stg[:, slot * 1024: slot * 1024 + dram_ap.shape[1]], dram_ap), dq, 16)
            wV(gl)
            gc = inc(V.tensor_copy(dest_ap, stg[:, slot * 1024: slot * 1024 + dram_ap.shape[1]]), sV)
            r_stg[slot] = gc
            return gc

        r_ab = [None, None]
        OT_f = OT[:].bitcast(F32)
        AB = [OT_f[:, 4096:5120], OT_f[:, 5120:6144]]
        AB2 = [OT_f[:, 6144:7168], OT_f[:, 7168:8192]]
        r_ab2 = [None, None]
        g_wmat = {"v": None, "k": None, "q": None}

        def ab_load_convert(dram_ap, dest_ap, slot):
            wG(r_ab[slot])
            gl = inc(G.dma_start(AB[slot][:, 0:dram_ap.shape[1]], dram_ap),
                     dG if slot == 0 else dG2, 16)
            wV(gl)
            gc = inc(V.tensor_copy(dest_ap, AB[slot][:, 0:dram_ap.shape[1]]), sV)
            r_ab[slot] = gc
            return gc

        g_wqkv = (sP, sP.v)
        dump("d_wq", wq_b[:, 0:1024], g_wqkv)

        def emit_rest_weights():
            wG((dG, dG.v), (dG2, dG2.v))
            for c in range(8):
                ab_load_convert(wo_d[c * 128:(c + 1) * 128, :], wo_b[:, c * 1024:(c + 1) * 1024], c % 2)
            ab_load_convert(fa_d, fa_b[:], 0)
            wG(r_ab[1])
            gl = inc(G.dma_start(AB[1][0:16, :], fbb_d), dG2, 16)
            wV(gl)
            r_ab[1] = inc(V.tensor_copy(fbb_b[:], AB[1][0:16, :]), sV)
            wG(r_ab[0])
            gl = inc(G.dma_start(AB[0][0:1, :], fb_d), dG, 16)
            wV(gl)
            r_ab[0] = inc(V.tensor_copy(fbias_b[:], AB[0][0:1, :]), sV)

        qg = {"gla": [None] * 8, "gl": [None] * 8}

        def qa_load(c, slot):
            wG(r_ab[slot])
            qg["gla"][c] = inc(G.dma_start(AB[slot][:, 0:1024],
                                           wf_d[c * 128:(c + 1) * 128, :]),
                               dG if slot == 0 else dG2, 16)

        def qa_reduce(c, slot):
            wV(qg["gla"][c])
            r_ab[slot] = inc(V.tensor_reduce(wmax[:, c:c + 1], AB[slot],
                                             AX.X, ALU.max, apply_absolute_value=True), sV)

        def emit_quant_mid():
            wV(r_ab[0], r_ab[1])
            g_wm = inc(V.tensor_reduce(wmax[:, 8:9], wmax[:, 0:8], AX.X, ALU.max), sV)
            wG(g_wm)
            g_wb1 = inc(G.dma_start(scr_wm.transpose([1, 0]), wmax[:, 8:9]), dQ, 16)
            wG(g_wb1, r_ab[0])
            g_wb2 = inc(G.dma_start(AB[0][0:1, 0:128], scr_wm), dQ, 16)
            qg["wb2"] = g_wb2
            wV(g_wb2)
            g_t1 = inc(V.tensor_reduce(gsc[0:1, 0:1], AB[0][0:1, 0:128], AX.X, ALU.max), sV)
            r_ab[0] = g_t1
            wV(g_t1)
            g_t2 = inc(V.tensor_scalar(gsc[0:1, 1:2], gsc[0:1, 0:1], 1.0 / 7.0, None, ALU.mult), sV)
            wV(g_t2)
            g_sc = inc(V.reciprocal(gsc[0:1, 2:3], gsc[0:1, 1:2]), sV)
            wG(g_sc)
            g_d1 = inc(G.dma_start(scr_sc, gsc[0:1, 1:3]), dQ, 16)
            wG(g_d1)
            g_d2 = inc(G.dma_start(sclb[:], scr_sc.broadcast_to([128, 2])), dQ, 16)
            qg["d2"] = g_d2
            dump("d_sclb", sclb[:], g_d2)
            dump("d_gsc", gsc[:], g_sc)

        def qs_load(c, slot):
            wG(r_ab[0], r_ab[1], qg["wb2"])
            qg["gl"][c] = inc(G.dma_start(AB[0],
                                          wf_d[c * 128:(c + 1) * 128, :]), dG, 16)

        def qs_scale(c, slot):
            qs = AB[1]
            qs_i = AB[0].bitcast(I32)
            wV(qg["gl"][c], qg["d2"])
            gq = inc(V.tensor_scalar(qs, AB[0], sclb[:, 1:2], None, ALU.mult), sV)
            wV(gq)
            gq = inc(V.tensor_copy(qs_i, qs), sV)
            wV(gq)
            gq = inc(V.tensor_scalar(qs, qs_i, 7.0, -8.0, ALU.min, ALU.max), sV)
            wV(gq)
            gq = inc(V.tensor_scalar(wf_b[:, c * 1024:(c + 1) * 1024], qs,
                                     sclb[:, 0:1], None, ALU.mult), sV)
            r_ab[0] = gq
            r_ab[1] = gq

        def quant_hook(g_, st):
            if g_ == 1:
                if st == 0:
                    wG((dG, dG.v), (dG2, dG2.v))
                    qa_load(0, 0); qa_load(1, 1)
                elif st == 1:
                    qa_reduce(0, 0); qa_reduce(1, 1); qa_load(2, 0); qa_load(3, 1)
                elif st == 2:
                    qa_reduce(2, 0); qa_reduce(3, 1); qa_load(4, 0); qa_load(5, 1)
                else:
                    qa_reduce(4, 0); qa_reduce(5, 1); qa_load(6, 0); qa_load(7, 1)
            elif g_ == 2:
                if st == 0:
                    qa_reduce(6, 0); qa_reduce(7, 1)
                    emit_quant_mid()
                    qs_load(0, 0)
                elif st == 1:
                    qs_scale(0, 0); qs_load(1, 0); qs_scale(1, 0); qs_load(2, 0)
                elif st == 2:
                    qs_scale(2, 0); qs_load(3, 0); qs_scale(3, 0); qs_load(4, 0)
                else:
                    qs_scale(4, 0); qs_load(5, 0); qs_scale(5, 0); qs_load(6, 0)
            elif g_ == 3 and st == 0:
                qs_scale(6, 0); qs_load(7, 0); qs_scale(7, 0)
                dump("d_wfq", wf_b[:, 0:1024], r_ab[0])
                dump("d_wfall", wf_b[:], r_ab[0], r_ab[1])

        dump("d_wq", wq_b[:, 0:1024], g_wqkv)

        g_qst = [[None] * 4, [None] * 4]
        g_kall = [None, None]
        g_vt_ready = [None, None]
        g_vtslot = [None, None]
        g_vtslot_rd = [None, None]

        def qkv_tile(gp, side, mat_b, dst, st2, bk):
            wT(g_wqkv, g_wmat[dst], g_x0T_ready, g_pbV[bk], g_pb_readers[bk], g_pav_act[bk // 2])
            mm = None
            for mc in range(NMC):
                mm = T.matmul(pb[bk][:],
                              mat_b[:, mc * 1024 + gp * 128: mc * 1024 + (gp + 1) * 128],
                              x0T[:, mc * 2048 + st2 * 512: mc * 2048 + (st2 + 1) * 512],
                              start=(mc == 0), stop=(mc == NMC - 1))
            gm = inc(mm, sT)
            g_pb_readers[bk] = gm
            wV(gm)
            if dst == "v":
                slot = st2 % 2
                wV(g_vtslot_rd[slot])
                ge = inc(V.tensor_copy(vtT[:, slot * 512:(slot + 1) * 512], pb[bk][:]), sV)
                g_vtslot[slot] = ge
            elif dst == "q":
                ge = inc(V.tensor_copy(QT[side][:, st2 * 512:(st2 + 1) * 512], pb[bk][:]), sV)
                g_qst[side][st2] = ge
            else:
                ge = inc(V.tensor_copy(KT[side][:, st2 * 512:(st2 + 1) * 512], pb[bk][:]), sV)
                g_kall[side] = ge
            g_pbV[bk] = ge
            return ge

        def v_transp(side, tc2, bk):
            slot = (tc2 // 4) % 2
            pos = (tc2 % 4) * 128
            pbk_bf = pb[bk][:].bitcast(BF16)
            wT(g_ident, g_vtslot[slot], g_pbV[bk], g_pb_readers[bk], g_pav_act[bk // 2])
            gtr = inc(T.transpose(pbk_bf[:, pos: pos + 128],
                                  vtT[:, slot * 512 + pos: slot * 512 + pos + 128],
                                  ident_b[:]), sT)
            g_pb_readers[bk] = gtr
            g_vtslot_rd[slot] = gtr
            wV(gtr)
            V.tensor_copy(VTILE[side][:, tc2 * 130: tc2 * 130 + 64], pbk_bf[:, pos: pos + 64])
            gcv = inc(V.tensor_copy(VTILE[side][:, tc2 * 130 + 65: tc2 * 130 + 129],
                                    pbk_bf[:, pos + 64: pos + 128]), sV)
            g_pbV[bk] = gcv
            g_vt_ready[side] = gcv
            return gcv

        def window_items(gp, side, st):
            W0, W1 = 2 * (1 - (st % 2)), 2 * (1 - (st % 2)) + 1
            if st == 0:
                pre = [(vtile_ones, (side,))] if side == 1 else []
                return pre + [(qkv_tile, (gp, side, wv_b, "v", 0, W0)),
                        (qkv_tile, (gp, side, wv_b, "v", 1, W1)),
                        (qkv_tile, (gp, side, wk_b, "k", 0, W0))]
            if st == 1:
                return [(v_transp, (side, 0, W1)), (v_transp, (side, 1, W1)),
                        (v_transp, (side, 2, W1)), (v_transp, (side, 3, W1)),
                        (qkv_tile, (gp, side, wv_b, "v", 2, W0)),
                        (qkv_tile, (gp, side, wk_b, "k", 1, W1)),
                        (v_transp, (side, 4, W0)), (v_transp, (side, 5, W0)),
                        (v_transp, (side, 6, W0)), (v_transp, (side, 7, W0))]
            if st == 2:
                return [(qkv_tile, (gp, side, wv_b, "v", 3, W0)),
                        (qkv_tile, (gp, side, wk_b, "k", 2, W1)),
                        (qkv_tile, (gp, side, wk_b, "k", 3, W0)),
                        (v_transp, (side, 8, W1)), (v_transp, (side, 9, W1)),
                        (v_transp, (side, 10, W1)), (v_transp, (side, 11, W1))]
            return [(v_transp, (side, 12, W1)), (v_transp, (side, 13, W1)),
                    (v_transp, (side, 14, W1)), (v_transp, (side, 15, W1)),
                    (qkv_tile, (gp, side, wq_b, "q", 0, W0)),
                    (qkv_tile, (gp, side, wq_b, "q", 1, W0)),
                    (qkv_tile, (gp, side, wq_b, "q", 2, W1)),
                    (qkv_tile, (gp, side, wq_b, "q", 3, W0))]

        for b in range(nseq):
            wV((oS, oS.v))
            if b > 0:
                wG((oS, oS.v))
            g_xchunk = [None] * NMC
            qsi = {"SY": 0, "G": 0, "SC": 0}

            def q_issue(qn, dram_ap, extra=()):
                sl = qsi[qn] % 2
                qsi[qn] += 1
                w = dram_ap.shape[1]
                if qn == "SY":
                    wSY(r_stg[sl], *extra)
                    gl = inc(SY.dma_start(stg[:, sl * 1024: sl * 1024 + w], dram_ap),
                             dS if sl == 0 else dS2, 16)
                elif qn == "G":
                    wG(r_ab2[sl], *extra)
                    gl = inc(G.dma_start(AB2[sl][:, 0:w], dram_ap),
                             dG if sl == 0 else dG2, 16)
                else:
                    wSC(r_ab[sl], *extra)
                    gl = inc(SC.dma_start(AB[sl][:, 0:w], dram_ap),
                             dC if sl == 0 else dC2, 16)
                return gl, sl

            def q_conv(qn, gl, sl, dest_ap, extra=()):
                wV(gl, *extra)
                w = dest_ap.shape[1]
                src = (stg[:, sl * 1024: sl * 1024 + w] if qn == "SY"
                       else (AB2 if qn == "G" else AB)[sl][:, 0:w])
                gc = inc(V.tensor_copy(dest_ap, src), sV)
                (r_stg if qn == "SY" else (r_ab2 if qn == "G" else r_ab))[sl] = gc
                return gc

            for mc in range(NMC):
                gl0, s0 = q_issue("SY", xT_d[b, mc * 128:(mc + 1) * 128, 0:1024],
                                  (g_vtile_readers,))
                gl1, s1 = q_issue("G", xT_d[b, mc * 128:(mc + 1) * 128, 1024:2048])
                if b == 0:
                    gl2, s2 = q_issue("SC", wv_d[mc * 128:(mc + 1) * 128, :])
                q_conv("SY", gl0, s0, x0T[:, mc * 2048: mc * 2048 + 1024],
                       (g_x0T_readers,))
                gc = q_conv("G", gl1, s1, x0T[:, mc * 2048 + 1024:(mc + 1) * 2048],
                            (g_x0T_readers,))
                if b == 0:
                    gc = q_conv("SC", gl2, s2, wv_b[:, mc * 1024:(mc + 1) * 1024])
                g_xchunk[mc] = gc
                wT(g_wqkv, g_xchunk[mc], g_qkv_readers, g_vtile_readers)
                for st2 in range(4):
                    if mc == 0:
                        wT(g_pbV[st2], g_pb_readers[st2], g_pav_act[st2 // 2])
                    mm = T.matmul(pb[st2][:],
                                  wv_b[:, mc * 1024: mc * 1024 + 128],
                                  x0T[:, mc * 2048 + st2 * 512: mc * 2048 + (st2 + 1) * 512],
                                  start=(mc == 0), stop=(mc == NMC - 1),
                                  skip_group_check=True)
                    if mc == NMC - 1:
                        gm = inc(mm, sT)
                        g_pb_readers[st2] = gm
                        if st2 < 2:
                            wV(gm, g_vtslot_rd[st2])
                            ge = inc(V.tensor_copy(vtT[:, st2 * 512:(st2 + 1) * 512],
                                                   pb[st2][:]), sV)
                            g_vtslot[st2] = ge
                            g_pbV[st2] = ge
            g_x0T_ready = (sV, sV.v)
            if b == 0:
                dump("d_x0T", x0T[:, 0:2048], g_x0T_ready)
                for qn, c in (("SY", 0), ("G", 1), ("SC", 6), ("SY", 2), ("G", 3),
                              ("SC", 7), ("SY", 4), ("G", 5)):
                    gl, sl = q_issue(qn, wk_d[c * 128:(c + 1) * 128, :])
                    q_conv(qn, gl, sl, wk_b[:, c * 1024:(c + 1) * 1024])
                g_wmat["k"] = (sV, sV.v)
                for qn, c in (("SY", 0), ("G", 1), ("SC", 6), ("SY", 2), ("G", 3),
                              ("SC", 7), ("SY", 4), ("G", 5)):
                    gl, sl = q_issue(qn, wq_d[c * 128:(c + 1) * 128, :])
                    q_conv(qn, gl, sl, wq_b[:, c * 1024:(c + 1) * 1024])
                g_wmat["q"] = (sV, sV.v)

            bkc = [0]

            def nextbk():
                bkc[0] = (bkc[0] + 1) % 4
                return bkc[0]

            for tc2 in range(8):
                v_transp(0, tc2, tc2 % 2)
            for st2 in (2, 3):
                wV((sT, sT.v), g_vtslot_rd[st2 % 2])
                ge = inc(V.tensor_copy(vtT[:, (st2 % 2) * 512:(st2 % 2 + 1) * 512],
                                       pb[st2][:]), sV)
                g_vtslot[st2 % 2] = ge
                g_pbV[st2] = ge
            for tc2 in range(8, 16):
                v_transp(0, tc2, nextbk())
            for st2 in range(4):
                qkv_tile(0, 0, wk_b, "k", st2, nextbk())
            for st2 in range(4):
                qkv_tile(0, 0, wq_b, "q", st2, nextbk())
            if b == 0:
                dump("d_qT", QT[0], g_qst[0][3])
                dump("d_kT", KT[0], g_kall[0])
                dump("d_vt", VTILE[0], g_vt_ready[0])

            pend_norm = [None, None]
            for g in range(npair):
                side = g % 2
                qT_s, kT_s, vtile_s = QT[side], KT[side], VTILE[side]

                def emit_normalize(st, g_=None, gates=None):
                    g2 = g if g_ is None else g_
                    fp = st % 2
                    pav0, pav1 = pb[2 * fp], pb[2 * fp + 1]
                    g_av = (gates or g_av_gates)[st]
                    nonlocal g_rd_reader, g_otmp_reader
                    wSC(g_av, g_rd_reader)
                    inc(SC.activation(rd[64:65, 0:512], pav0[64:65, :], AF.Ln), sA)
                    g_ln = inc(SC.activation(rd[64:65, 512:1024], pav1[64:65, :], AF.Ln), sA)
                    wSC(g_ln)
                    g_rd = inc(SC.activation(rd[64:65, 1024:2048], rd[64:65, 0:1024],
                                             AF.Exp, scale=-1.0), sA)
                    g_pav_act[fp] = g_rd
                    wSY(g_rd, g_scr_slot[st % 2])
                    g_b1 = inc(SY.dma_start(scr_rd[st % 2:st % 2 + 1, :], rd[64:65, 1024:2048]), dB, 16)
                    g_rd_reader = g_b1
                    wSY(g_b1, g_rdb_reader[0])
                    g_b2 = inc(SY.dma_start(rdb[:, 0:1024],
                                            scr_rd[st % 2:st % 2 + 1, :].broadcast_to([64, 1024])), dB3, 16)
                    g_scr_slot[st % 2] = g_b2
                    wV(g_b2, g_OT_readers, qg.get("done"), r_ab[0], r_ab[1])
                    gm0 = inc(V.tensor_tensor(OT[0:64, g2 * 2048 + st * 512: g2 * 2048 + (st + 1) * 512],
                                              pav0[0:64, :], rdb[:, 0:512],
                                              ALU.mult), sV)
                    wV(g_otmp_reader)
                    gm1 = inc(V.tensor_tensor(otmp, pav1[0:64, :],
                                              rdb[:, 512:1024], ALU.mult), sV)
                    g_rdb_reader[0] = gm1
                    g_pbV[2 * fp] = gm1
                    g_pbV[2 * fp + 1] = gm1
                    wSY(gm1)
                    g_mv = inc(SY.dma_start(OT[64:128, g2 * 2048 + st * 512: g2 * 2048 + (st + 1) * 512],
                                            otmp), dB2, 16)
                    g_otmp_reader = g_mv

                g_av_gates = [None] * nfrag
                NI = nfrag * NTC
                g_sc_gates = [None] * (NI + 1)
                g_exp_gates = [None] * NI
                witems = []
                wpos = 0

                def emit_scores(i):
                    st, p = i // NTC, i % 2
                    P = ps0 if p == 0 else ps1
                    wT(g_lastexp[p], g_kall[side], g_qst[side][st])
                    T.matmul(P[:, 0:512],
                             kT_s[0:64, (i % NTC) * 128:(i % NTC + 1) * 128],
                             qT_s[0:64, st * 512:(st + 1) * 512],
                             start=True, stop=True, tile_position=(0, 0))
                    mm = T.matmul(P[:, 512:1024],
                                  kT_s[64:128, (i % NTC) * 128:(i % NTC + 1) * 128],
                                  qT_s[64:128, st * 512:(st + 1) * 512],
                                  start=True, stop=True, tile_position=(64, 0))
                    g_sc_gates[i] = inc(mm, sT)

                def emit_av(i):
                    st, tc, p = i // NTC, i % NTC, i % 2
                    fp = st % 2
                    pav0, pav1 = pb[2 * fp], pb[2 * fp + 1]
                    wT(g_exp_gates[i], g_vt_ready[side])
                    if tc == 0:
                        wT(g_pbV[2 * fp], g_pbV[2 * fp + 1], g_pav_act[fp],
                           g_pb_readers[2 * fp], g_pb_readers[2 * fp + 1])
                    T.matmul(pav0[0:65, :], vtile_s[:, tc * 130: tc * 130 + 65],
                             Ebuf[:, p * 1024: p * 1024 + 512],
                             start=(tc == 0), stop=(tc == NTC - 1), skip_group_check=True)
                    mm = T.matmul(pav1[0:65, :], vtile_s[:, tc * 130 + 65: tc * 130 + 130],
                                  Ebuf[:, p * 1024 + 512: (p + 1) * 1024],
                                  start=(tc == 0), stop=(tc == NTC - 1), skip_group_check=True)
                    if tc == NTC - 1:
                        g_av_gates[st] = inc(mm, sT)

                emit_scores(0)
                for i in range(NI):
                    st, tc, p = i // NTC, i % NTC, i % 2
                    if tc == 0 and g + 1 < npair:
                        witems.extend(window_items(g + 1, 1 - side, st))
                    P = ps0 if p == 0 else ps1
                    wSC(g_sc_gates[i])
                    g_exp_gates[i] = inc(
                        SC.activation(Ebuf[:, p * 1024:(p + 1) * 1024], P[:, 0:1024],
                                      AF.Exp, scale=SCALE), sA)
                    g_lastexp[p] = g_exp_gates[i]
                    if i >= 1:
                        emit_av(i - 1)
                    if i + 1 < NI:
                        emit_scores(i + 1)
                    if tc == 1:
                        if st > 0:
                            emit_normalize(st - 1)
                        elif pend_norm[0] is not None:
                            emit_normalize(nfrag - 1, g_=pend_norm[0], gates=pend_norm[1])
                            pend_norm[0] = None
                    if tc == 4 and b == 0:
                        quant_hook(g, st)
                    if tc >= 10 and wpos < len(witems):
                        for _ in range(2):
                            if wpos < len(witems):
                                fn, args = witems[wpos]
                                fn(*args)
                                wpos += 1
                    if tc == NTC - 1:
                        assert wpos == len(witems), (g, st, wpos, len(witems))
                emit_av(NI - 1)
                while wpos < len(witems):
                    fn, args = witems[wpos]
                    fn(*args)
                    wpos += 1
                fp = (nfrag - 1) % 2
                g_pb_readers[2 * fp] = g_av_gates[nfrag - 1]
                g_pb_readers[2 * fp + 1] = g_av_gates[nfrag - 1]
                if b == 0 and g == 0:
                    dump("d_E", Ebuf[:, 1024:2048], g_exp_gates[NTC - 1])
                if g + 1 < npair:
                    pend_norm[0] = g
                    pend_norm[1] = g_av_gates
                else:
                    emit_normalize(nfrag - 1)
                g_qkv_readers = (sT, sT.v)
                g_vtile_readers = (sT, sT.v)
                if b == 0 and g == 0:
                    emit_rest_weights()
            g_OT_ready_V = (sV, sV.v)
            g_OT_ready_G = (dG, dG.v)
            g_norm_dB = (dB, dB.v)
            g_norm_dB3 = (dB3, dB3.v)
            g_norm_dB2 = (dB2, dB2.v)
            if b == 0:
                dump("d_OT", OT[:], g_OT_ready_V, g_OT_ready_G, g_norm_dB)

            sq_scr = Ebuf[:].bitcast(F32)
            g_wo_gates = [None] * nsb
            g_xr_gates = [None] * nsb

            def xres_prefetch(sbk):
                p = sbk % 2
                wSY(r_stg[p], g_vtile_readers)
                g_xr_gates[sbk] = inc(SY.dma_start(stg[:, p * 1024:(p + 1) * 1024],
                                                   x_d[b, sbk * 128:(sbk + 1) * 128, :]),
                                      dS if p == 0 else dS2, 16)

            def part1(sbk):
                p = sbk % 2
                P = ps0 if p == 0 else ps1
                if sbk == 0:
                    xres_prefetch(0)
                if sbk + 1 < nsb:
                    xres_prefetch(sbk + 1)
                wT(g_OT_ready_V)
                wt(T, "T", g_OT_ready_G)
                wt(T, "T", g_norm_dB)
                wt(T, "T", g_norm_dB3)
                wt(T, "T", g_norm_dB2)
                wT(g_lastexp[p], g_ln1add[p])
                for fc in range(8):
                    for half in range(2):
                        mm = T.matmul(P[:, half * 512:(half + 1) * 512],
                                      OT[:, fc * 2048 + sbk * 128: fc * 2048 + (sbk + 1) * 128],
                                      wo_b[:, fc * 1024 + half * 512: fc * 1024 + (half + 1) * 512],
                                      start=(fc == 0), stop=(fc == 7), skip_group_check=True)
                g_wo_gates[sbk] = inc(mm, sT)
                wV(g_wo_gates[sbk], g_xr_gates[sbk], qg.get("done"), r_ab[0], r_ab[1])
                g_add1 = inc(V.scalar_tensor_tensor(Bblk[:, p * 1024:(p + 1) * 1024], P[:],
                                                    1.0, stg[:, p * 1024:(p + 1) * 1024],
                                                    ALU.mult, ALU.add,
                                                    accum_out=lnsc[:, p * 8: p * 8 + 1]), sV)
                g_ln1add[p] = g_add1
                r_stg[p] = g_add1
                wV(g_add1)
                g_nm = inc(V.tensor_scalar(lnsc[:, p * 8 + 1: p * 8 + 2], lnsc[:, p * 8: p * 8 + 1],
                                           -1.0 / D, None, ALU.mult), sV)
                wSC(g_nm)
                g_sq = inc(SC.activation(sq_scr[:, 0:1024], Bblk[:, p * 1024:(p + 1) * 1024], AF.Square,
                                          bias=lnsc[:, p * 8 + 1: p * 8 + 2],
                                          accum_out=lnsc[:, p * 8 + 2: p * 8 + 3]), sA)
                wSC(g_sq)
                g_lnv = inc(SC.activation(lnsc[:, p * 8 + 3: p * 8 + 4], lnsc[:, p * 8 + 2: p * 8 + 3],
                                          AF.Ln, bias=epsc[:, 0:1], scale=1.0 / D), sA)
                wSC(g_lnv)
                g_rstd = inc(SC.activation(lnsc[:, p * 8 + 4: p * 8 + 5], lnsc[:, p * 8 + 3: p * 8 + 4],
                                           AF.Exp, scale=-0.5), sA)
                return (g_nm, g_rstd)

            def part2a(sbk, g_nm, g_rstd):
                p = sbk % 2
                wV(g_rstd, g_nm)
                g_x1 = inc(V.tensor_scalar(Ablk[:, p * 1024:(p + 1) * 1024],
                                           Bblk[:, p * 1024:(p + 1) * 1024],
                                           lnsc[:, p * 8 + 1: p * 8 + 2],
                                           lnsc[:, p * 8 + 4: p * 8 + 5],
                                           ALU.add, ALU.mult), sV)
                return g_x1

            def part2b(sbk, g_x1):
                p = sbk % 2
                for mc in range(8):
                    bank = pb[mc // 4]
                    if mc % 4 == 0:
                        wT(g_ident, g_x1, g_pbV[mc // 4], g_pb_readers[mc // 4], g_xa_evac[0],
                           g_pav_act[0])
                    gtr = inc(T.transpose(bank[:, (mc % 4) * 128:(mc % 4 + 1) * 128],
                                          Ablk[:, p * 1024 + mc * 128: p * 1024 + (mc + 1) * 128],
                                          ident_f[:]), sT)
                    g_pb_readers[mc // 4] = gtr
                wV(gtr)
                V.tensor_copy(x1T[:, 0:512], pb[0][:])
                g_x1T = inc(V.tensor_copy(x1T[:, 512:1024], pb[1][:]), sV)
                g_pbV[0] = g_x1T
                g_pbV[1] = g_x1T
                wT(g_x1T, g_pbV[2], g_pb_readers[2], g_pav_act[1])
                mm = None
                for mc in range(8):
                    mm = T.matmul(pb[2][0:16, 0:128], fa_b[:, mc * 16:(mc + 1) * 16],
                                  x1T[:, mc * 128:(mc + 1) * 128],
                                  start=(mc == 0), stop=(mc == 7))
                g_xamm = inc(mm, sT)
                g_pb_readers[2] = g_xamm
                wV(g_xamm)
                g_xa = inc(V.tensor_copy(xabuf[0:16, :], pb[2][0:16, 0:128]), sV)
                g_xa_evac[0] = g_xa
                g_pbV[2] = g_xa
                if b == 0 and sbk == 0:
                    dump("d_x1", Ablk[:, 0:1024], g_x1)
                    dump("d_x1T", x1T[:], g_x1T)
                    dump("d_xa", xabuf[0:16, :], g_xa)
                for half in range(2):
                    bank = pb[2 + half]
                    wT(g_ident, g_xa, g_pbV[2 + half], g_pb_readers[2 + half])
                    mm = None
                    for mc in range(8):
                        mm = T.matmul(bank[:], x1T[:, mc * 128:(mc + 1) * 128],
                                      wf_b[:, mc * 1024 + half * 512: mc * 1024 + (half + 1) * 512],
                                      start=(mc == 0), stop=False)
                    mm = T.matmul(bank[:], xabuf[0:16, :],
                                  fbb_b[:, half * 512:(half + 1) * 512], start=False, stop=False)
                    mm = T.matmul(bank[:], ones_b[:], fbias_b[:, half * 512:(half + 1) * 512],
                                  start=False, stop=True)
                    gf = inc(mm, sT)
                    g_pb_readers[2 + half] = gf
                    wV(gf)
                    ga = inc(V.scalar_tensor_tensor(
                        Bblk[:, p * 1024 + half * 512: p * 1024 + (half + 1) * 512], bank[:],
                        1.0, Ablk[:, p * 1024 + half * 512: p * 1024 + (half + 1) * 512],
                        ALU.mult, ALU.add,
                        accum_out=lnsc[:, p * 8 + 5 + half: p * 8 + 6 + half]), sV)
                    g_pbV[2 + half] = ga
                g_x1gate[0] = ga
                wV(ga)
                g_s2 = inc(V.tensor_tensor(lnsc[:, p * 8: p * 8 + 1], lnsc[:, p * 8 + 5: p * 8 + 6],
                                           lnsc[:, p * 8 + 6: p * 8 + 7], ALU.add), sV)
                wV(g_s2)
                g_nm2 = inc(V.tensor_scalar(lnsc[:, p * 8 + 1: p * 8 + 2], lnsc[:, p * 8: p * 8 + 1],
                                            -1.0 / D, None, ALU.mult), sV)
                wSC(g_nm2)
                g_sq2 = inc(SC.activation(sq_scr[:, 0:1024], Bblk[:, p * 1024:(p + 1) * 1024], AF.Square,
                                           bias=lnsc[:, p * 8 + 1: p * 8 + 2],
                                           accum_out=lnsc[:, p * 8 + 2: p * 8 + 3]), sA)
                wSC(g_sq2)
                g_lnv2 = inc(SC.activation(lnsc[:, p * 8 + 3: p * 8 + 4], lnsc[:, p * 8 + 2: p * 8 + 3],
                                           AF.Ln, bias=epsc[:, 0:1], scale=1.0 / D), sA)
                wSC(g_lnv2)
                g_rstd2 = inc(SC.activation(lnsc[:, p * 8 + 4: p * 8 + 5],
                                            lnsc[:, p * 8 + 3: p * 8 + 4], AF.Exp, scale=-0.5), sA)
                return (g_nm2, g_rstd2)

            def part3(sbk, g_nm2, g_rstd2):
                nonlocal g_outdma
                p = sbk % 2
                wV(g_rstd2, g_nm2, g_qkv_readers, g_outdma)
                g_x2 = inc(V.tensor_scalar(outblk[:], Bblk[:, p * 1024:(p + 1) * 1024],
                                           lnsc[:, p * 8 + 1: p * 8 + 2],
                                           lnsc[:, p * 8 + 4: p * 8 + 5],
                                           ALU.add, ALU.mult), sV)
                wSY(g_x2)
                g_outdma = inc(SY.dma_start(out_d[b, sbk * 128:(sbk + 1) * 128, :],
                                            outblk[:]), oS, 16)

            q2b = []
            q3 = []
            for sbk in range(nsb):
                if q3:
                    part3(*q3.pop(0))
                h = part1(sbk)
                gx1 = part2a(sbk, *h)
                if q2b:
                    k2, g2 = q2b.pop(0)
                    q3.append((k2,) + part2b(k2, g2))
                q2b.append((sbk, gx1))
            while q2b:
                if q3:
                    part3(*q3.pop(0))
                k2, g2 = q2b.pop(0)
                q3.append((k2,) + part2b(k2, g2))
            while q3:
                part3(*q3.pop(0))
            g_lastexp[0] = g_ln1add[0]
            g_lastexp[1] = g_ln1add[1]
            g_OT_readers = (sT, sT.v)
            g_x0T_readers = (sT, sT.v)
        SY.wait_ge(oS.h, oS.v)
    return nc


def _host_layout(L0_attn_x, L0_attn_kernel, L0_attn_WO, L0_ffn_kernel, L0_ffn_bias,
                 L0_ffn_a, L0_ffn_b):
    x = np.ascontiguousarray(np.asarray(L0_attn_x, dtype=np.float32))
    kern = np.asarray(L0_attn_kernel, dtype=np.float32)
    wq = np.ascontiguousarray(kern[0].transpose(1, 0, 2).reshape(D, D))
    wk = np.ascontiguousarray(kern[1].transpose(1, 0, 2).reshape(D, D))
    wv = np.ascontiguousarray(kern[2].transpose(1, 0, 2).reshape(D, D))
    wo = np.ascontiguousarray(np.asarray(L0_attn_WO, dtype=np.float32)[0]
                              .reshape(DH, NH, D).transpose(1, 0, 2).reshape(D, D))
    wf = np.ascontiguousarray(np.asarray(L0_ffn_kernel, dtype=np.float32))
    fb = np.ascontiguousarray(np.asarray(L0_ffn_bias, dtype=np.float32).reshape(1, D))
    fa = np.ascontiguousarray(np.asarray(L0_ffn_a, dtype=np.float32)
                              .reshape(8, 128, 16).transpose(1, 0, 2).reshape(128, 128))
    fbb = np.ascontiguousarray(np.asarray(L0_ffn_b, dtype=np.float32))
    shared = dict(wq=wq, wk=wk, wv=wv, wo=wo, wf=wf, fb=fb, fa=fa, fbb=fbb)
    in_maps = []
    for c in range(8):
        xs = np.ascontiguousarray(x[2 * c: 2 * c + 2])
        xTs = np.ascontiguousarray(xs.transpose(0, 2, 1))
        in_maps.append(dict(x=xs, xT=xTs, **shared))
    return in_maps


def kernel(L0_attn_x, L0_attn_kernel, L0_attn_WO, L0_ffn_kernel, L0_ffn_bias,
           L0_ffn_a, L0_ffn_b):
    in_maps = _host_layout(L0_attn_x, L0_attn_kernel, L0_attn_WO, L0_ffn_kernel,
                           L0_ffn_bias, L0_ffn_a, L0_ffn_b)
    nc = build_program()
    res = run_bass_kernel_spmd(nc, in_maps, core_ids=list(range(8)))
    return np.concatenate([r["out"] for r in res.results], axis=0)


# version 38
# speedup vs baseline: 1.2072x; 1.1964x over previous
import numpy as np
from contextlib import ExitStack

import concourse.bass as bass
import concourse.mybir as mybir
from concourse.bass_utils import run_bass_kernel_spmd

F32 = mybir.dt.float32
BF16 = mybir.dt.bfloat16
I32 = mybir.dt.int32
AF = mybir.ActivationFunctionType
ALU = mybir.AluOpType
AX = mybir.AxisListType

NSEQ = 2
S = 2048
D = 1024
NH = 16
DH = 64
NPAIR = 8
NMC = 8
NTC = 16
SF = 512
NFRAG = S // SF
NSB = 16
SCALE = 1.0 / 65536.0
EPS = 1e-5


class _Sem:
    __slots__ = ("h", "v")

    def __init__(self, h):
        self.h = h
        self.v = 0


def build_program(dumps=(), nseq=NSEQ, npair=NPAIR, nfrag=NFRAG, nsb=NSB, attn_stage="full"):
    dumps = set(dumps)
    nc = bass.Bass("TRN2", target_bir_lowering=False, debug=False)

    x_d = nc.dram_tensor("x", [NSEQ, S, D], F32, kind="ExternalInput").ap()
    xT_d = nc.dram_tensor("xT", [NSEQ, D, S], F32, kind="ExternalInput").ap()
    wq_d = nc.dram_tensor("wq", [D, D], F32, kind="ExternalInput").ap()
    wk_d = nc.dram_tensor("wk", [D, D], F32, kind="ExternalInput").ap()
    wv_d = nc.dram_tensor("wv", [D, D], F32, kind="ExternalInput").ap()
    wo_d = nc.dram_tensor("wo", [D, D], F32, kind="ExternalInput").ap()
    wf_d = nc.dram_tensor("wf", [D, D], F32, kind="ExternalInput").ap()
    fb_d = nc.dram_tensor("fb", [1, D], F32, kind="ExternalInput").ap()
    fa_d = nc.dram_tensor("fa", [128, 128], F32, kind="ExternalInput").ap()
    fbb_d = nc.dram_tensor("fbb", [16, D], F32, kind="ExternalInput").ap()
    out_d = nc.dram_tensor("out", [NSEQ, S, D], F32, kind="ExternalOutput").ap()
    scr_rd = nc.dram_tensor("scr_rd", [2, 1024], F32, kind="Internal").ap()
    scr_sc = nc.dram_tensor("scr_sc", [1, 2], F32, kind="Internal").ap()
    scr_wm = nc.dram_tensor("scr_wm", [1, 128], F32, kind="Internal").ap()

    dump_aps = {}
    for name, shape, dt in (
        ("d_qT", [128, S], BF16), ("d_kT", [128, S], BF16), ("d_vt", [128, 2080], BF16),
        ("d_E", [128, 1024], BF16), ("d_OT", [128, 16384], BF16),
        ("d_x1", [128, D], F32), ("d_x1T", [128, 1024], BF16),
        ("d_wq", [128, 1024], BF16), ("d_sclb", [128, 2], F32),
        ("d_wfq", [128, 1024], BF16), ("d_xa", [16, 128], BF16),
        ("d_x0T", [128, 2048], BF16), ("d_wmax", [128, 8], F32), ("d_gsc", [1, 4], F32),
        ("d_Bpre", [128, 1024], F32), ("d_wfall", [128, 8192], BF16),
        ("d_row", [1, 128], F32),
    ):
        if name in dumps:
            dump_aps[name] = nc.dram_tensor(name, shape, dt, kind="ExternalOutput").ap()

    ctx = ExitStack()
    with ctx:
        sb = lambda n, sh, dt: ctx.enter_context(nc.sbuf_tensor(n, sh, dt))
        ps = lambda n, sh, dt: ctx.enter_context(nc.psum_tensor(n, sh, dt))

        stg = sb("stg", [128, 2048], F32)
        wq_b = sb("wq_b", [128, 8192], BF16)
        wk_b = sb("wk_b", [128, 8192], BF16)
        wv_b = sb("wv_b", [128, 8192], BF16)
        wo_b = sb("wo_b", [128, 8192], BF16)
        wf_b = sb("wf_b", [128, 8192], BF16)
        fa_b = sb("fa_b", [128, 128], BF16)
        fbb_b = sb("fbb_b", [16, 1024], BF16)
        fbias_b = sb("fbias_b", [1, 1024], BF16)
        ones_b = sb("ones_b", [1, 128], BF16)
        ident_b = sb("ident_b", [128, 128], BF16)
        ident_f = sb("ident_f", [128, 128], F32)
        x0T = sb("x0T", [128, 16384], BF16)
        qTa = sb("qTa", [128, 2048], BF16)
        kTa = sb("kTa", [128, 2048], BF16)
        vtila = sb("vtila", [128, 2080], BF16)
        Ebuf = sb("Ebuf", [128, 2048], BF16)
        OT = sb("OT", [128, 16384], BF16)
        rd = sb("rd", [128, 2048], F32)
        rdb = sb("rdb", [64, 1024], F32)
        xabuf = sb("xabuf", [128, 128], BF16)
        x1T = sb("x1T", [128, 1024], BF16)
        vtT = x1T
        vtT2 = xabuf
        Ablk = sb("Ablk", [128, 2048], F32)
        Bblk = sb("Bblk", [128, 2048], F32)
        outblk = sb("outblk", [128, 1024], F32)
        lnsc = sb("lnsc", [128, 32], F32)
        epsc = sb("epsc", [128, 1], F32)
        wmax = sb("wmax", [128, 9], F32)
        gsc = sb("gsc", [1, 4], F32)
        sclb = sb("sclb", [128, 2], F32)

        ps0 = ps("ps0", [128, 1024], F32)
        ps1 = ps("ps1", [128, 1024], F32)
        pb = [ps(f"pb{i}", [128, 512], F32) for i in range(4)]
        ps0_bf = ps0[:].bitcast(BF16)
        otmp = rd[:].bitcast(BF16)[0:64, 0:512]
        vtilb = stg[:].bitcast(BF16)[:, 0:2080]
        qTb = Bblk[:].bitcast(BF16)[:, 0:2048]
        kTb = outblk[:].bitcast(BF16)[:, 0:2048]
        QT = [qTa[:], qTb]
        KT = [kTa[:], kTb]
        VTILE = [vtila[:], vtilb]

        sT = _Sem(ctx.enter_context(nc.semaphore("sT")))
        sV = _Sem(ctx.enter_context(nc.semaphore("sV")))
        sA = _Sem(ctx.enter_context(nc.semaphore("sA")))
        sP = _Sem(ctx.enter_context(nc.semaphore("sP")))
        dS = _Sem(ctx.enter_context(nc.semaphore("dS")))
        dG = _Sem(ctx.enter_context(nc.semaphore("dG")))
        dB = _Sem(ctx.enter_context(nc.semaphore("dB")))
        dC = _Sem(ctx.enter_context(nc.semaphore("dC")))
        oS = _Sem(ctx.enter_context(nc.semaphore("oS")))
        dS2 = _Sem(ctx.enter_context(nc.semaphore("dS2")))
        dG2 = _Sem(ctx.enter_context(nc.semaphore("dG2")))
        dC2 = _Sem(ctx.enter_context(nc.semaphore("dC2")))
        dB2 = _Sem(ctx.enter_context(nc.semaphore("dB2")))
        dB3 = _Sem(ctx.enter_context(nc.semaphore("dB3")))
        dQ = _Sem(ctx.enter_context(nc.semaphore("dQ")))

        T, V, SC, SY, G = nc.tensor, nc.vector, nc.scalar, nc.sync, nc.gpsimd
        WM = {"T": {}, "V": {}, "SC": {}, "SY": {}, "G": {}}

        def inc(instr, s, n=1):
            instr.then_inc(s.h, n)
            s.v += n
            return (s, s.v)

        def wt(eng, key, *gates):
            wm = WM[key]
            for g in gates:
                if g is None:
                    continue
                s, v = g
                if wm.get(id(s), 0) < v:
                    eng.wait_ge(s.h, v)
                    wm[id(s)] = v

        wT = lambda *g: wt(T, "T", *g)
        wV = lambda *g: wt(V, "V", *g)
        wSC = lambda *g: wt(SC, "SC", *g)
        wSY = lambda *g: wt(SY, "SY", *g)
        wG = lambda *g: wt(G, "G", *g)

        def dump(name, ap, *gates):
            if name in dump_aps:
                wSY(*gates)
                inc(SY.dma_start(dump_aps[name], ap), oS, 16)

        gi = inc(G.memset(ident_b[:], 0.0), sP)
        wG(gi)
        gi = inc(G.affine_select(out=ident_b[:], in_=ident_b[:], compare_op=ALU.not_equal,
                                 fill=1.0, base=0, pattern=[[-1, 128]], channel_multiplier=1), sP)
        gi2 = inc(G.memset(ident_f[:], 0.0), sP)
        wG(gi2)
        gi2 = inc(G.affine_select(out=ident_f[:], in_=ident_f[:], compare_op=ALU.not_equal,
                                  fill=1.0, base=0, pattern=[[-1, 128]], channel_multiplier=1), sP)
        g_init = inc(G.memset(ones_b[:], 1.0), sP)
        g_ident = g_init
        def vtile_ones(side):
            g = None
            for tc in range(NTC):
                V.memset(VTILE[side][:, tc * 130 + 64: tc * 130 + 65], 1.0)
                g = inc(V.memset(VTILE[side][:, tc * 130 + 129: tc * 130 + 130], 1.0), sV)
            return g

        vtile_ones(0)
        V.memset(epsc[:], EPS)
        g_vones = inc(V.memset(lnsc[:], 0.0), sV)

        g_lastexp = [g_vones, g_vones]
        g_ln1add = [None, None]
        g_pb_readers = [None, None, None, None]
        g_pbV = [None] * 4
        g_x0T_readers = None
        g_qkv_readers = None
        g_vtile_readers = None
        g_OT_readers = None
        g_rd_reader = None
        g_rdb_reader = [None, None]
        g_scr_slot = [None, None]
        g_pav_act = [None, None]
        g_outdma = None
        g_x1gate = [None]
        g_otmp_reader = None
        g_xa_evac = [None, None]

        r_stg = [None, None]

        def load_convert(dram_ap, dest_ap, slot, q=0):
            Q, Qk, dq = (SY, "SY", dS) if q == 0 else (G, "G", dG)
            wt(Q, Qk, r_stg[slot])
            gl = inc(Q.dma_start(stg[:, slot * 1024: slot * 1024 + dram_ap.shape[1]], dram_ap), dq, 16)
            wV(gl)
            gc = inc(V.tensor_copy(dest_ap, stg[:, slot * 1024: slot * 1024 + dram_ap.shape[1]]), sV)
            r_stg[slot] = gc
            return gc

        r_ab = [None, None]
        OT_f = OT[:].bitcast(F32)
        AB = [OT_f[:, 4096:5120], OT_f[:, 5120:6144]]
        AB2 = [OT_f[:, 6144:7168], OT_f[:, 7168:8192]]
        r_ab2 = [None, None]
        g_wmat = {"v": None, "k": None, "q": None}

        def ab_load_convert(dram_ap, dest_ap, slot):
            wG(r_ab[slot])
            gl = inc(G.dma_start(AB[slot][:, 0:dram_ap.shape[1]], dram_ap),
                     dG if slot == 0 else dG2, 16)
            wV(gl)
            gc = inc(V.tensor_copy(dest_ap, AB[slot][:, 0:dram_ap.shape[1]]), sV)
            r_ab[slot] = gc
            return gc

        g_wqkv = (sP, sP.v)
        dump("d_wq", wq_b[:, 0:1024], g_wqkv)

        def emit_rest_weights():
            wG((dG, dG.v), (dG2, dG2.v))
            for c in range(8):
                ab_load_convert(wo_d[c * 128:(c + 1) * 128, :], wo_b[:, c * 1024:(c + 1) * 1024], c % 2)
            ab_load_convert(fa_d, fa_b[:], 0)
            wG(r_ab[1])
            gl = inc(G.dma_start(AB[1][0:16, :], fbb_d), dG2, 16)
            wV(gl)
            r_ab[1] = inc(V.tensor_copy(fbb_b[:], AB[1][0:16, :]), sV)
            wG(r_ab[0])
            gl = inc(G.dma_start(AB[0][0:1, :], fb_d), dG, 16)
            wV(gl)
            r_ab[0] = inc(V.tensor_copy(fbias_b[:], AB[0][0:1, :]), sV)

        qg = {"gla": [None] * 8, "gl": [None] * 8}

        def qa_load(c, slot):
            wG(r_ab[slot])
            qg["gla"][c] = inc(G.dma_start(AB[slot][:, 0:1024],
                                           wf_d[c * 128:(c + 1) * 128, :]),
                               dG if slot == 0 else dG2, 16)

        def qa_reduce(c, slot):
            wV(qg["gla"][c])
            r_ab[slot] = inc(V.tensor_reduce(wmax[:, c:c + 1], AB[slot],
                                             AX.X, ALU.max, apply_absolute_value=True), sV)

        def emit_quant_mid():
            wV(r_ab[0], r_ab[1])
            g_wm = inc(V.tensor_reduce(wmax[:, 8:9], wmax[:, 0:8], AX.X, ALU.max), sV)
            wG(g_wm)
            g_wb1 = inc(G.dma_start(scr_wm.transpose([1, 0]), wmax[:, 8:9]), dQ, 16)
            wG(g_wb1, r_ab[0])
            g_wb2 = inc(G.dma_start(AB[0][0:1, 0:128], scr_wm), dQ, 16)
            qg["wb2"] = g_wb2
            wV(g_wb2)
            g_t1 = inc(V.tensor_reduce(gsc[0:1, 0:1], AB[0][0:1, 0:128], AX.X, ALU.max), sV)
            r_ab[0] = g_t1
            wV(g_t1)
            g_t2 = inc(V.tensor_scalar(gsc[0:1, 1:2], gsc[0:1, 0:1], 1.0 / 7.0, None, ALU.mult), sV)
            wV(g_t2)
            g_sc = inc(V.reciprocal(gsc[0:1, 2:3], gsc[0:1, 1:2]), sV)
            wG(g_sc)
            g_d1 = inc(G.dma_start(scr_sc, gsc[0:1, 1:3]), dQ, 16)
            wG(g_d1)
            g_d2 = inc(G.dma_start(sclb[:], scr_sc.broadcast_to([128, 2])), dQ, 16)
            qg["d2"] = g_d2
            dump("d_sclb", sclb[:], g_d2)
            dump("d_gsc", gsc[:], g_sc)

        def qs_load(c, slot):
            wG(r_ab[0], r_ab[1], qg["wb2"])
            qg["gl"][c] = inc(G.dma_start(AB[0],
                                          wf_d[c * 128:(c + 1) * 128, :]), dG, 16)

        def qs_scale(c, slot):
            qs = AB[1]
            qs_i = AB[0].bitcast(I32)
            wV(qg["gl"][c], qg["d2"])
            gq = inc(V.tensor_scalar(qs, AB[0], sclb[:, 1:2], None, ALU.mult), sV)
            wV(gq)
            gq = inc(V.tensor_copy(qs_i, qs), sV)
            wV(gq)
            gq = inc(V.tensor_scalar(qs, qs_i, 7.0, -8.0, ALU.min, ALU.max), sV)
            wV(gq)
            gq = inc(V.tensor_scalar(wf_b[:, c * 1024:(c + 1) * 1024], qs,
                                     sclb[:, 0:1], None, ALU.mult), sV)
            r_ab[0] = gq
            r_ab[1] = gq

        def quant_hook(g_, st):
            if g_ == 1:
                if st == 0:
                    wG((dG, dG.v), (dG2, dG2.v))
                    qa_load(0, 0); qa_load(1, 1)
                elif st == 1:
                    qa_reduce(0, 0); qa_reduce(1, 1); qa_load(2, 0); qa_load(3, 1)
                elif st == 2:
                    qa_reduce(2, 0); qa_reduce(3, 1); qa_load(4, 0); qa_load(5, 1)
                else:
                    qa_reduce(4, 0); qa_reduce(5, 1); qa_load(6, 0); qa_load(7, 1)
            elif g_ == 2:
                if st == 0:
                    qa_reduce(6, 0); qa_reduce(7, 1)
                    emit_quant_mid()
                    qs_load(0, 0)
                elif st == 1:
                    qs_scale(0, 0); qs_load(1, 0); qs_scale(1, 0); qs_load(2, 0)
                elif st == 2:
                    qs_scale(2, 0); qs_load(3, 0); qs_scale(3, 0); qs_load(4, 0)
                else:
                    qs_scale(4, 0); qs_load(5, 0); qs_scale(5, 0); qs_load(6, 0)
            elif g_ == 3 and st == 0:
                qs_scale(6, 0); qs_load(7, 0); qs_scale(7, 0)
                dump("d_wfq", wf_b[:, 0:1024], r_ab[0])
                dump("d_wfall", wf_b[:], r_ab[0], r_ab[1])

        dump("d_wq", wq_b[:, 0:1024], g_wqkv)

        g_qst = [[None] * 4, [None] * 4]
        g_kall = [None, None]
        g_vt_ready = [None, None]
        g_vtslot = [None, None]
        g_vtslot_rd = [None, None]

        def qkv_tile(gp, side, mat_b, dst, st2, bk):
            wT(g_wqkv, g_wmat[dst], g_x0T_ready, g_pbV[bk], g_pb_readers[bk], g_pav_act[bk // 2])
            mm = None
            for mc in range(NMC):
                mm = T.matmul(pb[bk][:],
                              mat_b[:, mc * 1024 + gp * 128: mc * 1024 + (gp + 1) * 128],
                              x0T[:, mc * 2048 + st2 * 512: mc * 2048 + (st2 + 1) * 512],
                              start=(mc == 0), stop=(mc == NMC - 1))
            gm = inc(mm, sT)
            g_pb_readers[bk] = gm
            wV(gm)
            if dst == "v":
                slot = st2 % 2
                wV(g_vtslot_rd[slot])
                ge = inc(V.tensor_copy(vtT[:, slot * 512:(slot + 1) * 512], pb[bk][:]), sV)
                g_vtslot[slot] = ge
            elif dst == "q":
                ge = inc(V.tensor_copy(QT[side][:, st2 * 512:(st2 + 1) * 512], pb[bk][:]), sV)
                g_qst[side][st2] = ge
            else:
                ge = inc(V.tensor_copy(KT[side][:, st2 * 512:(st2 + 1) * 512], pb[bk][:]), sV)
                g_kall[side] = ge
            g_pbV[bk] = ge
            return ge

        def v_transp(side, tc2, bk):
            slot = (tc2 // 4) % 2
            pos = (tc2 % 4) * 128
            pbk_bf = pb[bk][:].bitcast(BF16)
            wT(g_ident, g_vtslot[slot], g_pbV[bk], g_pb_readers[bk], g_pav_act[bk // 2])
            gtr = inc(T.transpose(pbk_bf[:, pos: pos + 128],
                                  vtT[:, slot * 512 + pos: slot * 512 + pos + 128],
                                  ident_b[:]), sT)
            g_pb_readers[bk] = gtr
            g_vtslot_rd[slot] = gtr
            wV(gtr)
            V.tensor_copy(VTILE[side][:, tc2 * 130: tc2 * 130 + 64], pbk_bf[:, pos: pos + 64])
            gcv = inc(V.tensor_copy(VTILE[side][:, tc2 * 130 + 65: tc2 * 130 + 129],
                                    pbk_bf[:, pos + 64: pos + 128]), sV)
            g_pbV[bk] = gcv
            g_vt_ready[side] = gcv
            return gcv

        def window_items(gp, side, st):
            W0, W1 = 2 * (1 - (st % 2)), 2 * (1 - (st % 2)) + 1
            if st == 0:
                pre = [(vtile_ones, (side,))] if side == 1 else []
                return pre + [(qkv_tile, (gp, side, wv_b, "v", 0, W0)),
                        (qkv_tile, (gp, side, wv_b, "v", 1, W1)),
                        (qkv_tile, (gp, side, wk_b, "k", 0, W0))]
            if st == 1:
                return [(v_transp, (side, 0, W1)), (v_transp, (side, 1, W1)),
                        (v_transp, (side, 2, W1)), (v_transp, (side, 3, W1)),
                        (qkv_tile, (gp, side, wv_b, "v", 2, W0)),
                        (qkv_tile, (gp, side, wk_b, "k", 1, W1)),
                        (v_transp, (side, 4, W0)), (v_transp, (side, 5, W0)),
                        (v_transp, (side, 6, W0)), (v_transp, (side, 7, W0))]
            if st == 2:
                return [(qkv_tile, (gp, side, wv_b, "v", 3, W0)),
                        (qkv_tile, (gp, side, wk_b, "k", 2, W1)),
                        (qkv_tile, (gp, side, wk_b, "k", 3, W0)),
                        (v_transp, (side, 8, W1)), (v_transp, (side, 9, W1)),
                        (v_transp, (side, 10, W1)), (v_transp, (side, 11, W1))]
            return [(v_transp, (side, 12, W1)), (v_transp, (side, 13, W1)),
                    (v_transp, (side, 14, W1)), (v_transp, (side, 15, W1)),
                    (qkv_tile, (gp, side, wq_b, "q", 0, W0)),
                    (qkv_tile, (gp, side, wq_b, "q", 1, W0)),
                    (qkv_tile, (gp, side, wq_b, "q", 2, W1)),
                    (qkv_tile, (gp, side, wq_b, "q", 3, W0))]

        for b in range(nseq):
            if b > 0:
                wG((oS, oS.v))
            g_xchunk = [None] * NMC
            qsi = {"SY": 0, "G": 0, "SC": 0}

            def q_issue(qn, dram_ap, extra=()):
                sl = qsi[qn] % 2
                qsi[qn] += 1
                w = dram_ap.shape[1]
                if qn == "SY":
                    wSY(r_stg[sl], *extra)
                    gl = inc(SY.dma_start(stg[:, sl * 1024: sl * 1024 + w], dram_ap),
                             dS if sl == 0 else dS2, 16)
                elif qn == "G":
                    wG(r_ab2[sl], *extra)
                    gl = inc(G.dma_start(AB2[sl][:, 0:w], dram_ap),
                             dG if sl == 0 else dG2, 16)
                else:
                    wSC(r_ab[sl], *extra)
                    gl = inc(SC.dma_start(AB[sl][:, 0:w], dram_ap),
                             dC if sl == 0 else dC2, 16)
                return gl, sl

            def q_conv(qn, gl, sl, dest_ap, extra=()):
                wV(gl, *extra)
                w = dest_ap.shape[1]
                src = (stg[:, sl * 1024: sl * 1024 + w] if qn == "SY"
                       else (AB2 if qn == "G" else AB)[sl][:, 0:w])
                gc = inc(V.tensor_copy(dest_ap, src), sV)
                (r_stg if qn == "SY" else (r_ab2 if qn == "G" else r_ab))[sl] = gc
                return gc

            for mc in range(NMC):
                gl0, s0 = q_issue("SY", xT_d[b, mc * 128:(mc + 1) * 128, 0:1024],
                                  (g_vtile_readers,))
                gl1, s1 = q_issue("G", xT_d[b, mc * 128:(mc + 1) * 128, 1024:2048])
                if b == 0:
                    gl2, s2 = q_issue("SC", wv_d[mc * 128:(mc + 1) * 128, :])
                q_conv("SY", gl0, s0, x0T[:, mc * 2048: mc * 2048 + 1024],
                       (g_x0T_readers,))
                gc = q_conv("G", gl1, s1, x0T[:, mc * 2048 + 1024:(mc + 1) * 2048],
                            (g_x0T_readers,))
                if b == 0:
                    gc = q_conv("SC", gl2, s2, wv_b[:, mc * 1024:(mc + 1) * 1024])
                g_xchunk[mc] = gc
                wT(g_wqkv, g_xchunk[mc], g_qkv_readers, g_vtile_readers)
                for st2 in range(4):
                    if mc == 0:
                        wT(g_pbV[st2], g_pb_readers[st2], g_pav_act[st2 // 2])
                    mm = T.matmul(pb[st2][:],
                                  wv_b[:, mc * 1024: mc * 1024 + 128],
                                  x0T[:, mc * 2048 + st2 * 512: mc * 2048 + (st2 + 1) * 512],
                                  start=(mc == 0), stop=(mc == NMC - 1),
                                  skip_group_check=True)
                    if mc == NMC - 1:
                        gm = inc(mm, sT)
                        g_pb_readers[st2] = gm
                        if st2 < 2:
                            wV(gm, g_vtslot_rd[st2])
                            ge = inc(V.tensor_copy(vtT[:, st2 * 512:(st2 + 1) * 512],
                                                   pb[st2][:]), sV)
                            g_vtslot[st2] = ge
                            g_pbV[st2] = ge
            g_x0T_ready = (sV, sV.v)
            if b == 0:
                dump("d_x0T", x0T[:, 0:2048], g_x0T_ready)
                for qn, c in (("SY", 0), ("G", 1), ("SC", 6), ("SY", 2), ("G", 3),
                              ("SC", 7), ("SY", 4), ("G", 5)):
                    gl, sl = q_issue(qn, wk_d[c * 128:(c + 1) * 128, :])
                    q_conv(qn, gl, sl, wk_b[:, c * 1024:(c + 1) * 1024])
                g_wmat["k"] = (sV, sV.v)
                for qn, c in (("SY", 0), ("G", 1), ("SC", 6), ("SY", 2), ("G", 3),
                              ("SC", 7), ("SY", 4), ("G", 5)):
                    gl, sl = q_issue(qn, wq_d[c * 128:(c + 1) * 128, :])
                    q_conv(qn, gl, sl, wq_b[:, c * 1024:(c + 1) * 1024])
                g_wmat["q"] = (sV, sV.v)

            bkc = [0]

            def nextbk():
                bkc[0] = (bkc[0] + 1) % 4
                return bkc[0]

            for tc2 in range(8):
                v_transp(0, tc2, tc2 % 2)
            for st2 in (2, 3):
                wV((sT, sT.v), g_vtslot_rd[st2 % 2])
                ge = inc(V.tensor_copy(vtT[:, (st2 % 2) * 512:(st2 % 2 + 1) * 512],
                                       pb[st2][:]), sV)
                g_vtslot[st2 % 2] = ge
                g_pbV[st2] = ge
            for tc2 in range(8, 16):
                v_transp(0, tc2, nextbk())
            for st2 in range(4):
                qkv_tile(0, 0, wk_b, "k", st2, nextbk())
            for st2 in range(4):
                qkv_tile(0, 0, wq_b, "q", st2, nextbk())
            if b == 0:
                dump("d_qT", QT[0], g_qst[0][3])
                dump("d_kT", KT[0], g_kall[0])
                dump("d_vt", VTILE[0], g_vt_ready[0])

            pend_norm = [None, None]
            for g in range(npair):
                side = g % 2
                qT_s, kT_s, vtile_s = QT[side], KT[side], VTILE[side]

                def emit_normalize(st, g_=None, gates=None):
                    g2 = g if g_ is None else g_
                    fp = st % 2
                    pav0, pav1 = pb[2 * fp], pb[2 * fp + 1]
                    g_av = (gates or g_av_gates)[st]
                    nonlocal g_rd_reader, g_otmp_reader
                    wSC(g_av, g_rd_reader)
                    inc(SC.activation(rd[64:65, 0:512], pav0[64:65, :], AF.Ln), sA)
                    g_ln = inc(SC.activation(rd[64:65, 512:1024], pav1[64:65, :], AF.Ln), sA)
                    wSC(g_ln)
                    g_rd = inc(SC.activation(rd[64:65, 1024:2048], rd[64:65, 0:1024],
                                             AF.Exp, scale=-1.0), sA)
                    g_pav_act[fp] = g_rd
                    wSY(g_rd, g_scr_slot[st % 2])
                    g_b1 = inc(SY.dma_start(scr_rd[st % 2:st % 2 + 1, :], rd[64:65, 1024:2048]), dB, 16)
                    g_rd_reader = g_b1
                    wSY(g_b1, g_rdb_reader[0])
                    g_b2 = inc(SY.dma_start(rdb[:, 0:1024],
                                            scr_rd[st % 2:st % 2 + 1, :].broadcast_to([64, 1024])), dB3, 16)
                    g_scr_slot[st % 2] = g_b2
                    wV(g_b2, g_OT_readers, qg.get("done"), r_ab[0], r_ab[1])
                    gm0 = inc(V.tensor_tensor(OT[0:64, g2 * 2048 + st * 512: g2 * 2048 + (st + 1) * 512],
                                              pav0[0:64, :], rdb[:, 0:512],
                                              ALU.mult), sV)
                    wV(g_otmp_reader)
                    gm1 = inc(V.tensor_tensor(otmp, pav1[0:64, :],
                                              rdb[:, 512:1024], ALU.mult), sV)
                    g_rdb_reader[0] = gm1
                    g_pbV[2 * fp] = gm1
                    g_pbV[2 * fp + 1] = gm1
                    wSY(gm1)
                    g_mv = inc(SY.dma_start(OT[64:128, g2 * 2048 + st * 512: g2 * 2048 + (st + 1) * 512],
                                            otmp), dB2, 16)
                    g_otmp_reader = g_mv

                g_av_gates = [None] * nfrag
                NI = nfrag * NTC
                g_sc_gates = [None] * (NI + 1)
                g_exp_gates = [None] * NI
                witems = []
                wpos = 0

                def emit_scores(i):
                    st, p = i // NTC, i % 2
                    P = ps0 if p == 0 else ps1
                    wT(g_lastexp[p], g_kall[side], g_qst[side][st])
                    T.matmul(P[:, 0:512],
                             kT_s[0:64, (i % NTC) * 128:(i % NTC + 1) * 128],
                             qT_s[0:64, st * 512:(st + 1) * 512],
                             start=True, stop=True, tile_position=(0, 0))
                    mm = T.matmul(P[:, 512:1024],
                                  kT_s[64:128, (i % NTC) * 128:(i % NTC + 1) * 128],
                                  qT_s[64:128, st * 512:(st + 1) * 512],
                                  start=True, stop=True, tile_position=(64, 0))
                    g_sc_gates[i] = inc(mm, sT)

                def emit_av(i):
                    st, tc, p = i // NTC, i % NTC, i % 2
                    fp = st % 2
                    pav0, pav1 = pb[2 * fp], pb[2 * fp + 1]
                    wT(g_exp_gates[i], g_vt_ready[side])
                    if tc == 0:
                        wT(g_pbV[2 * fp], g_pbV[2 * fp + 1], g_pav_act[fp],
                           g_pb_readers[2 * fp], g_pb_readers[2 * fp + 1])
                    T.matmul(pav0[0:65, :], vtile_s[:, tc * 130: tc * 130 + 65],
                             Ebuf[:, p * 1024: p * 1024 + 512],
                             start=(tc == 0), stop=(tc == NTC - 1), skip_group_check=True)
                    mm = T.matmul(pav1[0:65, :], vtile_s[:, tc * 130 + 65: tc * 130 + 130],
                                  Ebuf[:, p * 1024 + 512: (p + 1) * 1024],
                                  start=(tc == 0), stop=(tc == NTC - 1), skip_group_check=True)
                    if tc == NTC - 1:
                        g_av_gates[st] = inc(mm, sT)

                emit_scores(0)
                for i in range(NI):
                    st, tc, p = i // NTC, i % NTC, i % 2
                    if tc == 0 and g + 1 < npair:
                        witems.extend(window_items(g + 1, 1 - side, st))
                    P = ps0 if p == 0 else ps1
                    wSC(g_sc_gates[i])
                    g_exp_gates[i] = inc(
                        SC.activation(Ebuf[:, p * 1024:(p + 1) * 1024], P[:, 0:1024],
                                      AF.Exp, scale=SCALE), sA)
                    g_lastexp[p] = g_exp_gates[i]
                    if i >= 1:
                        emit_av(i - 1)
                    if i + 1 < NI:
                        emit_scores(i + 1)
                    if tc == 1:
                        if st > 0:
                            emit_normalize(st - 1)
                        elif pend_norm[0] is not None:
                            emit_normalize(nfrag - 1, g_=pend_norm[0], gates=pend_norm[1])
                            pend_norm[0] = None
                    if tc == 4 and b == 0:
                        quant_hook(g, st)
                    if tc >= 10 and wpos < len(witems):
                        for _ in range(2):
                            if wpos < len(witems):
                                fn, args = witems[wpos]
                                fn(*args)
                                wpos += 1
                    if tc == NTC - 1:
                        assert wpos == len(witems), (g, st, wpos, len(witems))
                emit_av(NI - 1)
                while wpos < len(witems):
                    fn, args = witems[wpos]
                    fn(*args)
                    wpos += 1
                fp = (nfrag - 1) % 2
                g_pb_readers[2 * fp] = g_av_gates[nfrag - 1]
                g_pb_readers[2 * fp + 1] = g_av_gates[nfrag - 1]
                if b == 0 and g == 0:
                    dump("d_E", Ebuf[:, 1024:2048], g_exp_gates[NTC - 1])
                if g + 1 < npair:
                    pend_norm[0] = g
                    pend_norm[1] = g_av_gates
                else:
                    emit_normalize(nfrag - 1)
                g_qkv_readers = (sT, sT.v)
                g_vtile_readers = (sT, sT.v)
                if b == 0 and g == 0:
                    emit_rest_weights()
            g_OT_ready_V = (sV, sV.v)
            g_OT_ready_G = (dG, dG.v)
            g_norm_dB = (dB, dB.v)
            g_norm_dB3 = (dB3, dB3.v)
            g_norm_dB2 = (dB2, dB2.v)
            if b == 0:
                dump("d_OT", OT[:], g_OT_ready_V, g_OT_ready_G, g_norm_dB)

            sq_scr = Ebuf[:].bitcast(F32)
            g_wo_gates = [None] * nsb
            g_xr_gates = [None] * nsb

            def xres_prefetch(sbk):
                p = sbk % 2
                wSY(r_stg[p], g_vtile_readers)
                g_xr_gates[sbk] = inc(SY.dma_start(stg[:, p * 1024:(p + 1) * 1024],
                                                   x_d[b, sbk * 128:(sbk + 1) * 128, :]),
                                      dS if p == 0 else dS2, 16)

            def part1(sbk):
                p = sbk % 2
                P = ps0 if p == 0 else ps1
                if sbk == 0:
                    xres_prefetch(0)
                if sbk + 1 < nsb:
                    xres_prefetch(sbk + 1)
                wT(g_OT_ready_V)
                wt(T, "T", g_OT_ready_G)
                wt(T, "T", g_norm_dB)
                wt(T, "T", g_norm_dB3)
                wt(T, "T", g_norm_dB2)
                wT(g_lastexp[p], g_ln1add[p])
                for fc in range(8):
                    for half in range(2):
                        mm = T.matmul(P[:, half * 512:(half + 1) * 512],
                                      OT[:, fc * 2048 + sbk * 128: fc * 2048 + (sbk + 1) * 128],
                                      wo_b[:, fc * 1024 + half * 512: fc * 1024 + (half + 1) * 512],
                                      start=(fc == 0), stop=(fc == 7), skip_group_check=True)
                g_wo_gates[sbk] = inc(mm, sT)
                wV(g_wo_gates[sbk], g_xr_gates[sbk], qg.get("done"), r_ab[0], r_ab[1])
                g_add1 = inc(V.scalar_tensor_tensor(Bblk[:, p * 1024:(p + 1) * 1024], P[:],
                                                    1.0, stg[:, p * 1024:(p + 1) * 1024],
                                                    ALU.mult, ALU.add,
                                                    accum_out=lnsc[:, p * 8: p * 8 + 1]), sV)
                g_ln1add[p] = g_add1
                r_stg[p] = g_add1
                wV(g_add1)
                g_nm = inc(V.tensor_scalar(lnsc[:, p * 8 + 1: p * 8 + 2], lnsc[:, p * 8: p * 8 + 1],
                                           -1.0 / D, None, ALU.mult), sV)
                wSC(g_nm)
                g_sq = inc(SC.activation(sq_scr[:, 0:1024], Bblk[:, p * 1024:(p + 1) * 1024], AF.Square,
                                          bias=lnsc[:, p * 8 + 1: p * 8 + 2],
                                          accum_out=lnsc[:, p * 8 + 2: p * 8 + 3]), sA)
                wSC(g_sq)
                g_lnv = inc(SC.activation(lnsc[:, p * 8 + 3: p * 8 + 4], lnsc[:, p * 8 + 2: p * 8 + 3],
                                          AF.Ln, bias=epsc[:, 0:1], scale=1.0 / D), sA)
                wSC(g_lnv)
                g_rstd = inc(SC.activation(lnsc[:, p * 8 + 4: p * 8 + 5], lnsc[:, p * 8 + 3: p * 8 + 4],
                                           AF.Exp, scale=-0.5), sA)
                return (g_nm, g_rstd)

            def part2a(sbk, g_nm, g_rstd):
                p = sbk % 2
                wV(g_rstd, g_nm)
                g_x1 = inc(V.tensor_scalar(Ablk[:, p * 1024:(p + 1) * 1024],
                                           Bblk[:, p * 1024:(p + 1) * 1024],
                                           lnsc[:, p * 8 + 1: p * 8 + 2],
                                           lnsc[:, p * 8 + 4: p * 8 + 5],
                                           ALU.add, ALU.mult), sV)
                return g_x1

            def part2b(sbk, g_x1):
                p = sbk % 2
                for mc in range(8):
                    bank = pb[mc // 4]
                    if mc % 4 == 0:
                        wT(g_ident, g_x1, g_pbV[mc // 4], g_pb_readers[mc // 4], g_xa_evac[0],
                           g_pav_act[0])
                    gtr = inc(T.transpose(bank[:, (mc % 4) * 128:(mc % 4 + 1) * 128],
                                          Ablk[:, p * 1024 + mc * 128: p * 1024 + (mc + 1) * 128],
                                          ident_f[:]), sT)
                    g_pb_readers[mc // 4] = gtr
                wV(gtr)
                V.tensor_copy(x1T[:, 0:512], pb[0][:])
                g_x1T = inc(V.tensor_copy(x1T[:, 512:1024], pb[1][:]), sV)
                g_pbV[0] = g_x1T
                g_pbV[1] = g_x1T
                wT(g_x1T, g_pbV[2], g_pb_readers[2], g_pav_act[1])
                mm = None
                for mc in range(8):
                    mm = T.matmul(pb[2][0:16, 0:128], fa_b[:, mc * 16:(mc + 1) * 16],
                                  x1T[:, mc * 128:(mc + 1) * 128],
                                  start=(mc == 0), stop=(mc == 7))
                g_xamm = inc(mm, sT)
                g_pb_readers[2] = g_xamm
                wV(g_xamm)
                g_xa = inc(V.tensor_copy(xabuf[0:16, :], pb[2][0:16, 0:128]), sV)
                g_xa_evac[0] = g_xa
                g_pbV[2] = g_xa
                if b == 0 and sbk == 0:
                    dump("d_x1", Ablk[:, 0:1024], g_x1)
                    dump("d_x1T", x1T[:], g_x1T)
                    dump("d_xa", xabuf[0:16, :], g_xa)
                for half in range(2):
                    bank = pb[2 + half]
                    wT(g_ident, g_xa, g_pbV[2 + half], g_pb_readers[2 + half])
                    mm = None
                    for mc in range(8):
                        mm = T.matmul(bank[:], x1T[:, mc * 128:(mc + 1) * 128],
                                      wf_b[:, mc * 1024 + half * 512: mc * 1024 + (half + 1) * 512],
                                      start=(mc == 0), stop=False)
                    mm = T.matmul(bank[:], xabuf[0:16, :],
                                  fbb_b[:, half * 512:(half + 1) * 512], start=False, stop=False)
                    mm = T.matmul(bank[:], ones_b[:], fbias_b[:, half * 512:(half + 1) * 512],
                                  start=False, stop=True)
                    gf = inc(mm, sT)
                    g_pb_readers[2 + half] = gf
                    wV(gf)
                    ga = inc(V.scalar_tensor_tensor(
                        Bblk[:, p * 1024 + half * 512: p * 1024 + (half + 1) * 512], bank[:],
                        1.0, Ablk[:, p * 1024 + half * 512: p * 1024 + (half + 1) * 512],
                        ALU.mult, ALU.add,
                        accum_out=lnsc[:, p * 8 + 5 + half: p * 8 + 6 + half]), sV)
                    g_pbV[2 + half] = ga
                g_x1gate[0] = ga
                wV(ga)
                g_s2 = inc(V.tensor_tensor(lnsc[:, p * 8: p * 8 + 1], lnsc[:, p * 8 + 5: p * 8 + 6],
                                           lnsc[:, p * 8 + 6: p * 8 + 7], ALU.add), sV)
                wV(g_s2)
                g_nm2 = inc(V.tensor_scalar(lnsc[:, p * 8 + 1: p * 8 + 2], lnsc[:, p * 8: p * 8 + 1],
                                            -1.0 / D, None, ALU.mult), sV)
                wSC(g_nm2)
                g_sq2 = inc(SC.activation(sq_scr[:, 0:1024], Bblk[:, p * 1024:(p + 1) * 1024], AF.Square,
                                           bias=lnsc[:, p * 8 + 1: p * 8 + 2],
                                           accum_out=lnsc[:, p * 8 + 2: p * 8 + 3]), sA)
                wSC(g_sq2)
                g_lnv2 = inc(SC.activation(lnsc[:, p * 8 + 3: p * 8 + 4], lnsc[:, p * 8 + 2: p * 8 + 3],
                                           AF.Ln, bias=epsc[:, 0:1], scale=1.0 / D), sA)
                wSC(g_lnv2)
                g_rstd2 = inc(SC.activation(lnsc[:, p * 8 + 4: p * 8 + 5],
                                            lnsc[:, p * 8 + 3: p * 8 + 4], AF.Exp, scale=-0.5), sA)
                return (g_nm2, g_rstd2)

            def part3(sbk, g_nm2, g_rstd2):
                nonlocal g_outdma
                p = sbk % 2
                wV(g_rstd2, g_nm2, g_qkv_readers, g_outdma)
                g_x2 = inc(V.tensor_scalar(outblk[:], Bblk[:, p * 1024:(p + 1) * 1024],
                                           lnsc[:, p * 8 + 1: p * 8 + 2],
                                           lnsc[:, p * 8 + 4: p * 8 + 5],
                                           ALU.add, ALU.mult), sV)
                wSY(g_x2)
                g_outdma = inc(SY.dma_start(out_d[b, sbk * 128:(sbk + 1) * 128, :],
                                            outblk[:]), oS, 16)

            q2b = []
            q3 = []
            for sbk in range(nsb):
                if q3:
                    part3(*q3.pop(0))
                h = part1(sbk)
                gx1 = part2a(sbk, *h)
                if q2b:
                    k2, g2 = q2b.pop(0)
                    q3.append((k2,) + part2b(k2, g2))
                q2b.append((sbk, gx1))
            while q2b:
                if q3:
                    part3(*q3.pop(0))
                k2, g2 = q2b.pop(0)
                q3.append((k2,) + part2b(k2, g2))
            while q3:
                part3(*q3.pop(0))
            g_lastexp[0] = g_ln1add[0]
            g_lastexp[1] = g_ln1add[1]
            g_OT_readers = (sT, sT.v)
            g_x0T_readers = (sT, sT.v)
        SY.wait_ge(oS.h, oS.v)
    return nc


def _host_layout(L0_attn_x, L0_attn_kernel, L0_attn_WO, L0_ffn_kernel, L0_ffn_bias,
                 L0_ffn_a, L0_ffn_b):
    x = np.ascontiguousarray(np.asarray(L0_attn_x, dtype=np.float32))
    kern = np.asarray(L0_attn_kernel, dtype=np.float32)
    wq = np.ascontiguousarray(kern[0].transpose(1, 0, 2).reshape(D, D))
    wk = np.ascontiguousarray(kern[1].transpose(1, 0, 2).reshape(D, D))
    wv = np.ascontiguousarray(kern[2].transpose(1, 0, 2).reshape(D, D))
    wo = np.ascontiguousarray(np.asarray(L0_attn_WO, dtype=np.float32)[0]
                              .reshape(DH, NH, D).transpose(1, 0, 2).reshape(D, D))
    wf = np.ascontiguousarray(np.asarray(L0_ffn_kernel, dtype=np.float32))
    fb = np.ascontiguousarray(np.asarray(L0_ffn_bias, dtype=np.float32).reshape(1, D))
    fa = np.ascontiguousarray(np.asarray(L0_ffn_a, dtype=np.float32)
                              .reshape(8, 128, 16).transpose(1, 0, 2).reshape(128, 128))
    fbb = np.ascontiguousarray(np.asarray(L0_ffn_b, dtype=np.float32))
    shared = dict(wq=wq, wk=wk, wv=wv, wo=wo, wf=wf, fb=fb, fa=fa, fbb=fbb)
    in_maps = []
    for c in range(8):
        xs = np.ascontiguousarray(x[2 * c: 2 * c + 2])
        xTs = np.ascontiguousarray(xs.transpose(0, 2, 1))
        in_maps.append(dict(x=xs, xT=xTs, **shared))
    return in_maps


def kernel(L0_attn_x, L0_attn_kernel, L0_attn_WO, L0_ffn_kernel, L0_ffn_bias,
           L0_ffn_a, L0_ffn_b):
    in_maps = _host_layout(L0_attn_x, L0_attn_kernel, L0_attn_WO, L0_ffn_kernel,
                           L0_ffn_bias, L0_ffn_a, L0_ffn_b)
    nc = build_program()
    res = run_bass_kernel_spmd(nc, in_maps, core_ids=list(range(8)))
    return np.concatenate([r["out"] for r in res.results], axis=0)
